# version 70
# baseline (speedup 1.0000x reference)
import contextlib
import numpy as np
import ml_dtypes
import concourse.bass as bass
from concourse.bass import IndirectOffsetOnAxis as IOA
import concourse.mybir as mybir
from concourse.bass_utils import run_bass_kernel_spmd

F32 = mybir.dt.float32
BF16 = mybir.dt.bfloat16
U32 = mybir.dt.uint32
AF = mybir.ActivationFunctionType
ALU = mybir.AluOpType
AX = mybir.AxisListType

ENGS = ("sync", "gpsimd", "tensor", "scalar", "vector")

GA = 0.7978845608028654
GB = 0.044715
N_ITER = 32


class Prog:

    def __init__(self, nc):
        self.nc = nc
        self.streams = {e: [] for e in ENGS}
        self.gmap = {e: {} for e in ENGS}
        self.sems = {}
        self.cnt = {}
        self.est = contextlib.ExitStack()

    def sem(self, name):
        h = self.est.enter_context(self.nc.semaphore(name))
        self.sems[name] = h
        self.cnt[name] = 0

    def op(self, eng, fn, waits=(), inc=None, n=1):
        waits = list(waits)
        if eng == "vector":
            waits.append(("V", self.cnt["V"]))
            inc, n = "V", 1
        elif eng == "scalar":
            waits.append(("A", self.cnt["A"]))
            inc, n = "A", 1
        wd = {}
        for s, v in waits:
            if v > 0:
                wd[s] = max(wd.get(s, 0), v)
        g = self.gmap[eng]
        wl = [(s, v) for s, v in sorted(wd.items()) if v > g.get(s, 0)]
        g.update(wl)
        self.streams[eng].append((fn, wl, inc, n))
        if inc is not None:
            self.cnt[inc] += n
        return self.cnt[inc] if inc is not None else None

    def snap(self):
        return dict(self.cnt)

    def replay(self, name, eng):
        sems = self.sems
        for fn, waits, inc, n in self.streams[name]:
            for s, v in waits[:-1]:
                eng.wait_ge(sems[s], v)
            inst = fn(eng)
            if waits:
                sv = waits[-1]
                inst._wait_ge(sems[sv[0]], sv[1])
            if inc is not None:
                inst.then_inc(sems[inc], n)

    def finish(self):
        with self.nc.Block() as blk:
            @blk.sync
            def _(eng):
                self.replay("sync", eng)

            @blk.gpsimd
            def _(eng):
                self.replay("gpsimd", eng)

            @blk.tensor
            def _(eng):
                self.replay("tensor", eng)

            @blk.scalar
            def _(eng):
                self.replay("scalar", eng)

            @blk.vector
            def _(eng):
                self.replay("vector", eng)


def build_program(S, D, F, R, AROT_OVERRIDE=None, QCH_OVERRIDE=None):
    DK = 64
    H = D // DK
    ST = S // 128
    DT = D // 128
    FT = F // 128
    ND5 = D // 512
    QCH = min(512, S) if QCH_OVERRIDE is None else QCH_OVERRIDE
    NQ = S // QCH
    NC5 = QCH // 512
    MT_B = 128
    NM_B = S // MT_B
    NSUB = 1
    K_SEL = S // 2
    SEL = S // 2
    NMS = SEL // 128
    AROT = min(4, ST) if AROT_OVERRIDE is None else AROT_OVERRIDE

    nc = bass.Bass("TRN2", target_bir_lowering=False)
    p = Prog(nc)
    est = p.est

    def din(name, shape, dt=F32):
        return nc.dram_tensor(name, shape, dt, kind="ExternalInput").ap()

    x_hbm = din("x", [R, S, D])
    out_hbm = nc.dram_tensor("out", [R, S, D], F32, kind="ExternalOutput").ap()
    wqT_h = din("wqT", [D, D])
    wkT_h = din("wkT", [D, D])
    wvT_h = din("wvT", [D, D])
    woTr_h = din("woTr", [128, H // 2, D])
    w1_h = din("w1r", [128, FT, DT, 128])
    w2_h = din("w2r", [128, FT, D])
    bq_h = din("bqr", [128, DT])
    bk_h = din("bkr", [128, DT])
    b1_h = din("b1r", [128, FT])
    bv_h = din("bvrep", [128, D])
    bo_h = din("borep", [128, D])
    b2_h = din("b2rep", [128, D])
    wr_h = din("wrrep", [128, D])
    id_h = din("ident", [128, 128], BF16)
    combd = nc.dram_tensor("combd", [S, 4], F32, kind="Internal").ap()
    pick_h = din("pick", [128, 128], BF16)

    DMAS = ("DC", "WQ", "WK", "WV", "WO", "XA0", "XA1", "XB0", "XB1",
            "BW0", "BW1", "BR0", "BR1", "OW0", "OW1", "DW0", "DW1",
            "CS", "CR", "XG0", "XG1", "SA0", "SA1", "BO", "OC0", "OC1")
    for s in ("P", "A", "V", "G") + DMAS:
        p.sem(s)

    sb = lambda name, shape, dt=F32: est.enter_context(nc.sbuf_tensor(name, shape, dt))

    bq_sb = sb("bq_sb", [128, DT])
    bk_sb = sb("bk_sb", [128, DT])
    b1_sb = sb("b1_sb", [128, FT])
    id_sb = sb("id_sb", [128, 128], BF16)
    ones64 = sb("ones64", [128, 64], BF16)
    pick_sb = sb("pick_sb", [128, 128], BF16)
    ones32 = sb("ones32", [1, 32])
    ssum = sb("ssum", [128, 32])
    ssq = sb("ssq", [128, 32])
    m_sb = sb("m_sb", [128, 32])
    var_sb = sb("var_sb", [128, 32])
    r_sb = sb("r_sb", [128, 32])
    nmr_sb = sb("nmr_sb", [128, 32])
    r1p_sb = sb("r1p_sb", [128, 32])
    scores = sb("scores", [128, 32])
    g_sb = sb("g_sb", [128, 32])
    s16 = sb("s16", [32, 128])
    th16 = sb("th16", [32, 128])
    g16 = sb("g16", [32, 128])
    gcnt = sb("gcnt", [32, 128])
    scr_a = sb("scr_a", [32, 32])
    scr_b = sb("scr_b", [32, 32])
    cnt_scr = sb("cnt_scr", [32, 32])
    tpad = sb("tpad", [32, 32])
    tT = sb("tT", [32, 32])
    sm = sb("sm", [1, 8])
    scA = sb("scA", [32, 128])
    scB = sb("scB", [32, 128])
    iotaF = sb("iotaF", [32, 128])
    gpos32 = sb("gpos32", [32, 128])
    itmp = sb("itmp", [128, 32])
    tokT = sb("tokT", [128, 32])
    comb_sb = sb("comb_sb", [128, 32, 4])
    offs_u = sb("offs_u", [128, 32], U32)
    idxu = sb("idxu", [128, 32], U32)
    comb2 = sb("comb2", [128, NMS, 4])
    xstO = sb("xstO", [128, 2, D])
    scrO = sb("scrO", [128, 1, D])
    wr_sb = sb("wr_sb", [128, D])
    LO, HI, TC, MK, CC, D1, D2, NM = 0, 1, 2, 3, 4, 5, 6, 7

    for dst, src in ((bq_sb, bq_h), (bk_sb, bk_h), (b1_sb, b1_h),
                     (id_sb, id_h), (pick_sb, pick_h), (wr_sb, wr_h)):
        p.op("sync", lambda e, d=dst, s=src: e.dma_start(out=d[:, :], in_=s),
             inc="DC", n=16)
    cDC = p.cnt["DC"]
    p.op("vector", lambda e: e.memset(ones64[:, :], 1.0), waits=[("DC", cDC)])
    p.op("vector", lambda e: e.memset(ones32[:, :], 1.0))
    p.op("vector", lambda e: e.memset(scA[:, :], 1.0))
    _cur, _oth = scA, scB
    for _sh in (1, 2, 4, 8, 16, 32, 64):
        p.op("vector", lambda e, sh=_sh, c=_cur, o=_oth: e.tensor_copy(
            o[:, 0:sh], c[:, 0:sh]))
        p.op("vector", lambda e, sh=_sh, c=_cur, o=_oth: e.tensor_add(
            o[:, sh:128], c[:, sh:128], c[:, 0:128 - sh]))
        _cur, _oth = _oth, _cur
    p.op("vector", lambda e, c=_cur: e.tensor_scalar(
        iotaF[:, :], c[:, :], 1.0, -1.0, ALU.mult, ALU.add))
    p.op("vector", lambda e: e.transpose(tT[:, :], iotaF[0:32, 0:32]))
    p.op("vector", lambda e: e.tensor_scalar(
        tpad[:, :], tT[:, :], 128.0, None, ALU.mult))
    p.op("vector", lambda e: e.tensor_single_scalar(
        gpos32[:, :], iotaF[:, :], tpad[:, 0:1], ALU.add))
    for _b in range(4):
        p.op("vector", lambda e, b=_b: e.transpose(
            itmp[32 * b:32 * b + 32, 0:32], iotaF[0:32, 32 * b:32 * b + 32]))
    for _ti in range(S // 128):
        p.op("vector", lambda e, ti=_ti: e.tensor_scalar(
            tokT[:, ti:ti + 1], itmp[:, 0:1], 1.0, float(128 * ti),
            ALU.mult, ALU.add))
    p.op("vector", lambda e: e.tensor_copy(
        comb_sb[:, 0:S // 128, 0:1],
        tokT[:, 0:S // 128].rearrange("p (t c) -> p t c", c=1)))

    prev_cr = [0]
    aov = {}
    rowtail = {}

    def make_A_quanta(rr):
        FBq = lambda sn: [(k, sn[k]) for k in ("P", "A", "V") + DMAS]
        aq = {}
        w0 = lambda: FBq(aq["snap0"])
        qs = []
        add = qs.append
        st_c = {}

        def q_snap():
            aq["snap0"] = p.snap()
        add(q_snap)

        def q_stat(i):
            w = w0() if i < 2 else [("V", st_c[("sq", i - 2)])]
            xsem = f"XA{i % 2}"
            c = p.op("sync", lambda e, i=i, sl=i % 2: e.dma_start(
                out=xstO[:, sl, :], in_=x_hbm[rr, i * 128:(i + 1) * 128, :]),
                waits=w, inc=xsem, n=16)
            st_c[("red", i)] = p.op("vector", lambda e, i=i, sl=i % 2:
                e.tensor_reduce(ssum[:, i:i + 1], xstO[:, sl, :], AX.X, ALU.add),
                waits=[(xsem, c)] + (w0() if i == 0 else []))
            p.op("vector", lambda e, sl=i % 2: e.tensor_mul(
                scrO[:, 0, :], xstO[:, sl, :], xstO[:, sl, :]),
                waits=[(xsem, c)])
            st_c[("sq", i)] = p.op("vector", lambda e, i=i: e.tensor_reduce(
                ssq[:, i:i + 1], scrO[:, 0, :], AX.X, ALU.add))
        for _i in range(ST):
            add(lambda i=_i: q_stat(i))

        def q_fin():
            inv_d = 1.0 / D
            p.op("vector", lambda e: e.tensor_scalar(
                m_sb[:, 0:ST], ssum[:, 0:ST], inv_d, None, ALU.mult), waits=w0())
            p.op("vector", lambda e: e.tensor_scalar(
                var_sb[:, 0:ST], ssq[:, 0:ST], inv_d, None, ALU.mult),
                waits=[("V", st_c[("sq", ST - 1)])])
            p.op("vector", lambda e: e.tensor_mul(
                scrO[:, 0, 0:ST], m_sb[:, 0:ST], m_sb[:, 0:ST]))
            p.op("vector", lambda e: e.tensor_sub(
                var_sb[:, 0:ST], var_sb[:, 0:ST], scrO[:, 0, 0:ST]))
            cV = p.op("vector", lambda e: e.tensor_scalar(
                scrO[:, 0, 0:ST], var_sb[:, 0:ST], 1.0, 1e-5, ALU.mult, ALU.add))
            cA_sq = p.op("scalar", lambda e: e.activation(
                r_sb[:, 0:ST], scrO[:, 0, 0:ST], AF.Sqrt),
                waits=[("V", cV)])
            p.op("vector", lambda e: e.reciprocal(r_sb[:, 0:ST], r_sb[:, 0:ST]),
                 waits=[("A", cA_sq)])
            for _ in range(2):
                p.op("vector", lambda e: e.tensor_mul(
                    scrO[:, 0, ST:2 * ST], r_sb[:, 0:ST], r_sb[:, 0:ST]))
                p.op("vector", lambda e: e.tensor_mul(
                    scrO[:, 0, ST:2 * ST], scrO[:, 0, ST:2 * ST], scrO[:, 0, 0:ST]))
                p.op("vector", lambda e: e.tensor_scalar(
                    scrO[:, 0, ST:2 * ST], scrO[:, 0, ST:2 * ST], -0.5, 1.5,
                    ALU.mult, ALU.add))
                p.op("vector", lambda e: e.tensor_mul(
                    r_sb[:, 0:ST], r_sb[:, 0:ST], scrO[:, 0, ST:2 * ST]))
            p.op("vector", lambda e: e.tensor_mul(
                nmr_sb[:, 0:ST], m_sb[:, 0:ST], r_sb[:, 0:ST]))
            p.op("vector", lambda e: e.tensor_scalar(
                nmr_sb[:, 0:ST], nmr_sb[:, 0:ST], -1.0, None, ALU.mult))
            p.op("vector", lambda e: e.tensor_scalar(
                r1p_sb[:, 0:ST], r_sb[:, 0:ST], 1.0, None, ALU.add))
            aq["cV_fin"] = p.cnt["V"]
            aq["cA_fin"] = p.cnt["A"]
        add(q_fin)

        def q_router(i):
            w = [("V", aq["cV_fin"]), ("A", aq["cA_fin"])] if i < 2 else \
                [("V", st_c[("amr", i - 2)])]
            xsem = f"XA{i % 2}"
            c = p.op("sync", lambda e, i=i, sl=i % 2: e.dma_start(
                out=xstO[:, sl, :], in_=x_hbm[rr, i * 128:(i + 1) * 128, :]),
                waits=w, inc=xsem, n=16)
            p.op("vector", lambda e, i=i, sl=i % 2: e.tensor_scalar(
                scrO[:, 0, :], xstO[:, sl, :], r_sb[:, i:i + 1],
                nmr_sb[:, i:i + 1], ALU.mult, ALU.add),
                waits=[(xsem, c), ("DC", cDC)])
            p.op("vector", lambda e: e.tensor_mul(
                scrO[:, 0, :], scrO[:, 0, :], wr_sb[:, :]))
            st_c[("amr", i)] = p.op("vector", lambda e, i=i: e.tensor_reduce(
                scores[:, i:i + 1], scrO[:, 0, :], AX.X, ALU.add))
        for _i in range(ST):
            add(lambda i=_i: q_router(i))

        def q_bis_init():
            p.op("vector", lambda e: e.memset(scores[:, ST:32], -3e38))
            p.op("vector", lambda e: e.memset(cnt_scr[:, :], 0.0))
            p.op("vector", lambda e: e.memset(scr_a[:, :], -3e38))
            cV = p.op("vector", lambda e: e.memset(scr_b[:, :], 3e38))
            for bi in range(4):
                cV = p.op("vector", lambda e, bi=bi: e.transpose(
                    s16[:, bi * 32:(bi + 1) * 32],
                    scores[bi * 32:(bi + 1) * 32, 0:32]))
            aq["cA_th"] = p.op("scalar", lambda e: e.activation(
                th16[0:32, :], s16[0:32, :], AF.Tanh, scale=0.5),
                waits=[("V", cV)] + w0())
            p.op("vector", lambda e: e.tensor_reduce(
                scr_a[0:ST, 0:1], s16[0:ST, :], AX.X, ALU.max))
            p.op("vector", lambda e: e.tensor_reduce(
                scr_b[0:ST, 0:1], s16[0:ST, :], AX.X, ALU.min))
            p.op("vector", lambda e: e.transpose(tpad[:, :], scr_a[:, :]))
            p.op("vector", lambda e: e.tensor_reduce(
                sm[:, HI:HI + 1], tpad[0:1, 0:32], AX.X, ALU.max))
            p.op("vector", lambda e: e.transpose(tpad[:, :], scr_b[:, :]))
            p.op("vector", lambda e: e.tensor_reduce(
                sm[:, LO:LO + 1], tpad[0:1, 0:32], AX.X, ALU.min))
            p.op("vector", lambda e: e.tensor_scalar(
                sm[:, HI:HI + 1], sm[:, HI:HI + 1], 1.0, None, ALU.add))
            p.op("vector", lambda e: e.tensor_scalar(
                sm[:, LO:LO + 1], sm[:, LO:LO + 1], -1.0, None, ALU.add))
            p.op("vector", lambda e: e.tensor_add(
                sm[:, TC:TC + 1], sm[:, LO:LO + 1], sm[:, HI:HI + 1]))
            p.op("vector", lambda e: e.tensor_scalar(
                sm[:, TC:TC + 1], sm[:, TC:TC + 1], 0.5, None, ALU.mult))
        add(q_bis_init)

        def bcast_t():
            p.op("vector", lambda e: e.tensor_single_scalar(
                tpad[0:1, :], ones32[:, :], sm[:, TC:TC + 1], ALU.mult))
            p.op("vector", lambda e: e.transpose(tT[:, :], tpad[:, :]))

        def q_bis_iter():
            bcast_t()
            p.op("vector", lambda e: e.tensor_single_scalar(
                gcnt[0:ST, :], s16[0:ST, :], tT[0:ST, 0:1], ALU.is_gt))
            p.op("vector", lambda e: e.tensor_reduce(
                cnt_scr[0:ST, 0:1], gcnt[0:ST, :], AX.X, ALU.add))
            p.op("vector", lambda e: e.transpose(tpad[:, :], cnt_scr[:, :]))
            p.op("vector", lambda e: e.tensor_reduce(
                sm[:, CC:CC + 1], tpad[0:1, 0:32], AX.X, ALU.add))
            p.op("vector", lambda e: e.tensor_single_scalar(
                sm[:, MK:MK + 1], sm[:, CC:CC + 1], float(K_SEL), ALU.is_ge))
            p.op("vector", lambda e: e.tensor_scalar(
                sm[:, NM:NM + 1], sm[:, MK:MK + 1], -1.0, 1.0, ALU.mult, ALU.add))
            p.op("vector", lambda e: e.tensor_sub(
                sm[:, D1:D1 + 1], sm[:, TC:TC + 1], sm[:, LO:LO + 1]))
            p.op("vector", lambda e: e.scalar_tensor_tensor(
                sm[:, LO:LO + 1], sm[:, D1:D1 + 1], sm[:, MK:MK + 1],
                sm[:, LO:LO + 1], ALU.mult, ALU.add))
            p.op("vector", lambda e: e.tensor_sub(
                sm[:, D2:D2 + 1], sm[:, TC:TC + 1], sm[:, HI:HI + 1]))
            p.op("vector", lambda e: e.scalar_tensor_tensor(
                sm[:, HI:HI + 1], sm[:, D2:D2 + 1], sm[:, NM:NM + 1],
                sm[:, HI:HI + 1], ALU.mult, ALU.add))
            p.op("vector", lambda e: e.tensor_add(
                sm[:, TC:TC + 1], sm[:, LO:LO + 1], sm[:, HI:HI + 1]))
            p.op("vector", lambda e: e.tensor_scalar(
                sm[:, TC:TC + 1], sm[:, TC:TC + 1], 0.5, None, ALU.mult))
        for _ in range(N_ITER):
            add(q_bis_iter)

        def q_gmask():
            p.op("vector", lambda e: e.tensor_copy(
                sm[:, TC:TC + 1], sm[:, LO:LO + 1]))
            bcast_t()
            p.op("vector", lambda e: e.tensor_single_scalar(
                gcnt[0:ST, :], s16[0:ST, :], tT[0:ST, 0:1], ALU.is_gt))
            if ST < 32:
                p.op("vector", lambda e: e.memset(g16[0:32, :], 0.0))
            p.op("vector", lambda e: e.tensor_scalar(
                g16[0:ST, :], th16[0:ST, :], 0.5, 0.5, ALU.mult, ALU.add),
                waits=[("A", aq["cA_th"])])
            p.op("vector", lambda e: e.tensor_mul(
                g16[0:ST, :], g16[0:ST, :], gcnt[0:ST, :]))
            for bi in range(4):
                p.op("vector", lambda e, bi=bi: e.transpose(
                    g_sb[bi * 32:(bi + 1) * 32, 0:32],
                    g16[0:32, bi * 32:(bi + 1) * 32]))
        add(q_gmask)

        def q_scan1():
            cur, oth = scA, scB
            p.op("vector", lambda e, c=cur: e.tensor_copy(
                c[0:ST, :], gcnt[0:ST, :]))
            for sh in (1, 2, 4, 8, 16, 32, 64):
                p.op("vector", lambda e, sh=sh, c=cur, o=oth: e.tensor_copy(
                    o[0:ST, 0:sh], c[0:ST, 0:sh]))
                p.op("vector", lambda e, sh=sh, c=cur, o=oth: e.tensor_add(
                    o[0:ST, sh:128], c[0:ST, sh:128], c[0:ST, 0:128 - sh]))
                cur, oth = oth, cur
            aq["rank"], aq["u"] = cur, oth
        add(q_scan1)

        def q_scan2():
            rank = aq["rank"]
            p.op("vector", lambda e: e.memset(tpad[:, :], 0.0))
            p.op("vector", lambda e, c=rank: e.tensor_copy(
                tpad[0:ST, 0:1], c[0:ST, 127:128]))
            p.op("vector", lambda e: e.transpose(tT[:, :], tpad[:, :]))
            a_, b_ = tT, tpad
            for sh in (1, 2, 4, 8, 16):
                p.op("vector", lambda e, sh=sh, a=a_, b=b_: e.tensor_copy(
                    b[0:1, 0:sh], a[0:1, 0:sh]))
                p.op("vector", lambda e, sh=sh, a=a_, b=b_: e.tensor_add(
                    b[0:1, sh:32], a[0:1, sh:32], a[0:1, 0:32 - sh]))
                a_, b_ = b_, a_
            p.op("vector", lambda e, b=b_: e.memset(b[0:1, 0:1], 0.0))
            p.op("vector", lambda e, a=a_, b=b_: e.tensor_copy(
                b[0:1, 1:32], a[0:1, 0:31]))
            p.op("vector", lambda e, a=a_, b=b_: e.transpose(a[:, :], b[:, :]))
            p.op("vector", lambda e, c=rank, roff=a_: e.tensor_single_scalar(
                c[0:ST, :], c[0:ST, :], roff[0:ST, 0:1], ALU.add))
        add(q_scan2)

        def q_offs():
            rank, u = aq["rank"], aq["u"]
            p.op("vector", lambda e, c=rank, u=u: e.tensor_sub(
                u[0:ST, :], c[0:ST, :], gpos32[0:ST, :]))
            p.op("vector", lambda e, u=u: e.tensor_scalar(
                u[0:ST, :], u[0:ST, :], 1.0, float(S - 1), ALU.mult, ALU.add))
            p.op("vector", lambda e, c=rank, u=u: e.tensor_sub(
                s16[0:ST, :], c[0:ST, :], u[0:ST, :]))
            p.op("vector", lambda e: e.tensor_scalar(
                s16[0:ST, :], s16[0:ST, :], 1.0, -1.0, ALU.mult, ALU.add))
            p.op("vector", lambda e: e.tensor_mul(
                s16[0:ST, :], s16[0:ST, :], gcnt[0:ST, :]))
            p.op("vector", lambda e, u=u: e.tensor_add(
                u[0:ST, :], u[0:ST, :], s16[0:ST, :]))
            for b in range(4):
                p.op("vector", lambda e, b=b, u=u: e.transpose(
                    itmp[32 * b:32 * b + 32, 0:32], u[0:32, 32 * b:32 * b + 32]))
            p.op("vector", lambda e: e.tensor_copy(
                offs_u[:, 0:ST], itmp[:, 0:ST]))
            p.op("vector", lambda e: e.tensor_copy(
                comb_sb[:, 0:ST, 1:2],
                nmr_sb[:, 0:ST].rearrange("p (t c) -> p t c", c=1)))
            p.op("vector", lambda e: e.tensor_copy(
                comb_sb[:, 0:ST, 2:3],
                r_sb[:, 0:ST].rearrange("p (t c) -> p t c", c=1)))
            aq["c_comb"] = p.op("vector", lambda e: e.tensor_copy(
                comb_sb[:, 0:ST, 3:4],
                g_sb[:, 0:ST].rearrange("p (t c) -> p t c", c=1)))
        add(q_offs)

        def q_scatter():
            for ti in range(ST):
                w = ([("V", aq["c_comb"]), ("CR", prev_cr[0])] + w0()) \
                    if ti == 0 else []
                p.op("gpsimd", lambda e, ti=ti: e.indirect_dma_start(
                    out=combd, out_offset=IOA(offs_u[:, ti:ti + 1], 0),
                    in_=comb_sb[:, ti, :], in_offset=None),
                    waits=w, inc="CS", n=16)
            ccr = p.op("sync", lambda e: e.dma_start(
                out=comb2[:, :, :],
                in_=combd[0:SEL].rearrange("(k p) c -> p k c", p=128)),
                waits=[("CS", p.cnt["CS"])], inc="CR", n=16)
            prev_cr[0] = p.cnt["CR"]
            aov[rr] = {"c_idx": p.op("vector", lambda e: e.tensor_copy(
                idxu[:, 0:NMS], comb2[:, :, 0:1].rearrange("p k c -> p (k c)")),
                waits=[("CR", ccr)])}
        add(q_scatter)
        return qs


    def emit_row(r):
        q0s, n0 = [], [0]
        if r == 0:
            q0s = make_A_quanta(0)
            while n0[0] < 2 + ST + 1:
                q0s[n0[0]]()
                n0[0] += 1

        def drain0(k):
            while n0[0] < len(q0s) and k > 0:
                q0s[n0[0]]()
                n0[0] += 1
                k -= 1
        nxt_qs = make_A_quanta(r + 1) if r + 1 < R else []
        nd = [0]

        def drain(k):
            while nd[0] < len(nxt_qs) and k > 0:
                nxt_qs[nd[0]]()
                nd[0] += 1
                k -= 1
        row_es = contextlib.ExitStack()
        xnT_es = contextlib.ExitStack()
        xn_T = xnT_es.enter_context(
            nc.sbuf_tensor(f"xn_T_r{r}", [128, DT, S], BF16, side="right"))
        bar0 = p.snap()
        FB = lambda sn: [(k, sn[k]) for k in ("P", "A", "V") + DMAS]
        FBW = lambda sn: [(k, sn[k]) for k in ("P", "A", "V") + DMAS
                          if k not in ("OW0", "OW1", "BR0", "BR1", "BO")]

        wes = contextlib.ExitStack()
        FTH = FT // 2
        assert FTH % 2 == 0 or FTH == 2 or True
        W1_sb = wes.enter_context(
            nc.sbuf_tensor(f"W1_sb_r{r}", [128, FTH, DT, 128], BF16))
        W2_sb = wes.enter_context(
            nc.sbuf_tensor(f"W2_sb_r{r}", [128, FTH, D], BF16))
        GJ = 4
        NG = (FTH + GJ - 1) // GJ
        grp_done = {}
        for g in range(NG):
            for jj in range(GJ * g, min(GJ * (g + 1), FTH)):
                w = []
                if jj == GJ * g:
                    if g != 0:
                        w = [grp_done[g - 1]]
                    elif r == 0:
                        w = FBW(bar0)
                    else:
                        w = ([("P", rowtail[r - 1]["P"]),
                              ("V", rowtail[r - 1]["V"]),
                              ("A", bar0["A"])] +
                             [(k, bar0[k]) for k in DMAS
                              if k not in ("OW0", "OW1", "BR0", "BR1", "BO")])
                gs = f"DW{g % 2}"
                p.op("gpsimd", lambda e, j=jj: e.dma_start(
                    out=W1_sb[:, j], in_=w1_h[:, j]), waits=w, inc=gs, n=16)
                c = p.op("gpsimd", lambda e, j=jj: e.dma_start(
                    out=W2_sb[:, j], in_=w2_h[:, j]), inc=gs, n=16)
            grp_done[g] = (gs, c)
        w1c = {(0, j): grp_done[j // GJ] for j in range(FTH)}
        w2c = {(0, j): grp_done[j // GJ] for j in range(FTH)}
        wch = {}
        for g in range(NG):
            wch[f"DW{g % 2}"] = grp_done[g]
        for k in ("DW0", "DW1"):
            wch.setdefault(k, grp_done[NG - 1])


        with contextlib.ExitStack() as bes:
            bsb = lambda nm, shp, dt=F32: bes.enter_context(
                nc.sbuf_tensor(f"{nm}_r{r}", shp, dt))
            b2_sb = bsb("b2_sb", [128, D])
            cb2 = p.op("sync", lambda e: e.dma_start(out=b2_sb[:, :], in_=b2_h),
                       waits=FB(bar0), inc="DC", n=16)
            h1_sb = bsb("h1_sb", [128, FT, MT_B], BF16)
            ys = bsb("ys", [128, 8, MT_B])
            t2s = bsb("t2s", [128, 4, MT_B])
            zs = bsb("zs", [128, 4, MT_B])
            ths = bsb("ths", [128, 4, MT_B])
            xstB = bsb("xstB", [128, 2, D])
            ostB = bsb("ostB", [128, 2, D])
            xg = bsb("xg", [128, 2, D])
            xnbg = bsb("xnbg", [128, 2, D], BF16)
            xng_T = bsb("xng_T", [128, DT, SEL], BF16)
            th0_sb = bsb("th0_sb", [128, NMS, D], BF16)
            ps_h = [bes.enter_context(nc.psum_tensor(f"ps_h{k}_r{r}", [128, 512]))
                    for k in range(4)]
            ps_y = [bes.enter_context(nc.psum_tensor(f"ps_y{k}_r{r}", [128, 512]))
                    for k in range(ND5)]
            ps_tb = [bes.enter_context(
                nc.psum_tensor(f"ps_tb{k}_r{r}", [128, 512], BF16))
                     for k in range(2)]

            xnbr = bsb("xnbr", [128, 2, D], BF16)
            psb_last = {0: None, 1: None}
            b0_c, bw_done, xtr_c = {}, {}, {}

            def emit_xtile(ti):
                xsem = f"XB{ti % 2}"
                if ti >= 2:
                    w = [("A", b0_c[ti - 2]), bw_done[ti - 2]]
                elif r == 0:
                    w = FB(bar0)
                else:
                    w = ([("P", rowtail[r - 1]["P"]),
                          ("V", rowtail[r - 1]["V"]), ("A", bar0["A"])] +
                         [(k, bar0[k]) for k in DMAS
                          if k not in ("OW0", "OW1", "BR0", "BR1", "BO")])
                c = p.op("sync", lambda e, ti=ti, sl=ti % 2: e.dma_start(
                    out=xstB[:, sl, :], in_=x_hbm[r, ti * 128:(ti + 1) * 128, :]),
                    waits=w, inc=xsem, n=16)
                wa = [(xsem, c)]
                if ti >= 2:
                    wa.append(("P", xtr_c[(ti - 2, DT - 1)]))
                p.op("scalar", lambda e, ti=ti, sl=ti % 2: e.activation(
                    xnbr[:, sl, :], xstB[:, sl, :], AF.Identity,
                    bias=nmr_sb[:, ti:ti + 1], scale=r_sb[:, ti:ti + 1]),
                    waits=wa)
                b0_c[ti] = p.op("scalar", lambda e, ti=ti, sl=ti % 2: e.activation(
                    xstB[:, sl, :], xstB[:, sl, :], AF.Identity,
                    bias=nmr_sb[:, ti:ti + 1], scale=r1p_sb[:, ti:ti + 1]))
                c2 = p.op("sync", lambda e, ti=ti, sl=ti % 2: e.dma_start(
                    out=out_hbm[r, ti * 128:(ti + 1) * 128, :],
                    in_=xstB[:, sl, :]),
                    waits=[("A", b0_c[ti])], inc=f"BW{ti % 2}", n=16)
                bw_done[ti] = (f"BW{ti % 2}", c2)
                for dt in range(DT):
                    w2 = [("A", b0_c[ti])] if dt == 0 else []
                    if ti == 0 and dt == 0:
                        w2 += FB(bar0)
                    if dt % 4 == 0 and psb_last[dt // 4] is not None:
                        w2.append(psb_last[dt // 4])
                    xtr_c[(ti, dt)] = p.op("tensor",
                        lambda e, ti=ti, dt=dt: e.transpose(
                            ps_tb[dt // 4][:, (dt % 4) * 128:(dt % 4 + 1) * 128],
                            xnbr[:, ti % 2, dt * 128:(dt + 1) * 128],
                            id_sb[:, :]), waits=w2, inc="P")
                for q in range(2):
                    c3 = p.op("vector", lambda e, ti=ti, q=q: e.tensor_copy(
                        xn_T[:, 4 * q:4 * q + 4, ti * 128:(ti + 1) * 128],
                        ps_tb[q][:, 0:512].rearrange("p (a t) -> p a t", t=128)),
                        waits=[("P", xtr_c[(ti, 4 * q + 3)])])
                    psb_last[q] = ("V", c3)


            xnbg_c, tpm_c, tpe_c = {}, {}, {}

            def prep(k):
                w = [("V", aov[r]["c_idx"])]
                if k >= 2:
                    w.append(("A", xnbg_c[k - 2]))
                cg = p.op("gpsimd", lambda e, k=k: e.indirect_dma_start(
                    out=xg[:, k % 2, :], out_offset=None,
                    in_=x_hbm.rearrange("r s d -> (r s) d"),
                    in_offset=IOA(idxu[:, k:k + 1], 0),
                    element_offset=r * S * D),
                    waits=w, inc=f"XG{k % 2}", n=16)
                wa = [(f"XG{k % 2}", cg)]
                if k >= 2:
                    wa.append(("P", tpm_c[(k - 2, DT - 1)]))
                xnbg_c[k] = p.op("scalar", lambda e, k=k: e.activation(
                    xnbg[:, k % 2, :], xg[:, k % 2, :], AF.Identity,
                    bias=comb2[:, k, 1:2], scale=comb2[:, k, 2:3]), waits=wa)
                for dt in range(DT):
                    w2 = [("A", xnbg_c[k])] if dt == 0 else []
                    if k == 0 and dt == 0:
                        w2 += FB(bar0)
                    if dt % 4 == 0 and psb_last[dt // 4] is not None:
                        w2.append(psb_last[dt // 4])
                    tpm_c[(k, dt)] = p.op("tensor",
                        lambda e, k=k, dt=dt: e.transpose(
                            ps_tb[dt // 4][:, (dt % 4) * 128:(dt % 4 + 1) * 128],
                            xnbg[:, k % 2, dt * 128:(dt + 1) * 128],
                            id_sb[:, :]), waits=w2, inc="P")
                for q in range(2):
                    tpe_c[(k, q)] = p.op("scalar", lambda e, k=k, q=q: e.activation(
                        xng_T[:, 4 * q:4 * q + 4, k * 128:(k + 1) * 128],
                        ps_tb[q][:, 0:512].rearrange("p (a t) -> p a t", t=128),
                        AF.Identity),
                        waits=[("P", tpm_c[(k, 4 * q + 3)])])
                    psb_last[q] = ("A", tpe_c[(k, q)])

            h1mm_c, h1bf_c, yev_c = {}, {}, {}
            sq_c2, t2_c2, z_c2, th_c2, w_c2 = {}, {}, {}, {}, {}
            o2mm_c, t1_c, gm_c, sa_done, fin_c = {}, {}, {}, {}, {}
            GSEQH = NMS * FTH
            GSEQ2 = 2 * GSEQH

            def gdec(g):
                ph, gg = divmod(g, GSEQH)
                m, jl = divmod(gg, FTH)
                return ph, m, jl, ph * FTH + jl, ph * NMS + m

            def st_h1mm(g):
                ph, m, jl, jg, M2 = gdec(g)
                sl = g % 4
                for k in range(DT):
                    w = []
                    if k == 0:
                        w = [w1c[(ph, jl)]]
                        if jl == 0:
                            w.append(("A", tpe_c[(m, 1)]))
                        if g >= 4:
                            w.append(("A", yev_c[g - 4]))
                    h1mm_c[g] = p.op("tensor",
                        lambda e, jl=jl, k=k, m=m, sl=sl: e.matmul(
                            ps_h[sl][:, 0:MT_B], W1_sb[:, jl, k],
                            xng_T[:, k, m * MT_B:(m + 1) * MT_B],
                            start=(k == 0), stop=(k == DT - 1)), waits=w, inc="P")
                if ph == 0 and m == NMS - 1:
                    gs = f"DW{jl % 2}"
                    c = p.op("gpsimd", lambda e, jl=jl: e.dma_start(
                        out=W1_sb[:, jl], in_=w1_h[:, FTH + jl]),
                        waits=[wch[gs], ("P", h1mm_c[g])], inc=gs, n=16)
                    wch[gs] = (gs, c)
                    w1c[(1, jl)] = (gs, c)

            def st_yev(g):
                ph, m, jl, jg, M2 = gdec(g)
                w = [("P", h1mm_c[g])]
                if g >= 8:
                    w += [("G", sq_c2[g - 8]), ("V", h1bf_c[g - 8])]
                yev_c[g] = p.op("scalar", lambda e, jg=jg, sl=g % 4, ysl=g % 8:
                    e.activation(ys[:, ysl, 0:MT_B], ps_h[sl][:, 0:MT_B],
                                 AF.Identity, bias=b1_sb[:, jg:jg + 1]), waits=w)

            def st_sq(g):
                if g % 2 == 0:
                    return
                ysl, tsl = (g - 1) % 8, (g - 1) % 4
                yv = ys[:, ysl:ysl + 2, :].rearrange("p a t -> p (a t)")
                tv = t2s[:, tsl:tsl + 2, :].rearrange("p a t -> p (a t)")
                w = [("A", yev_c[g]), ("G", p.cnt["G"])]
                if g >= 4:
                    w.append(("V", z_c2[g - 4]))
                c = p.op("gpsimd", lambda e, yv=yv, tv=tv: e.tensor_mul(
                    tv, yv, yv), waits=w, inc="G")
                sq_c2[g] = sq_c2[g - 1] = c
                c = p.op("vector", lambda e, tv=tv: e.tensor_scalar(
                    tv, tv, GA * GB, GA, ALU.mult, ALU.add),
                    waits=[("G", c)])
                t2_c2[g] = t2_c2[g - 1] = ("V", c)

            def st_z(g):
                if g % 2 == 0:
                    return
                ysl, tsl = (g - 1) % 8, (g - 1) % 4
                yv = ys[:, ysl:ysl + 2, :].rearrange("p a t -> p (a t)")
                tv = t2s[:, tsl:tsl + 2, :].rearrange("p a t -> p (a t)")
                zv = zs[:, tsl:tsl + 2, :].rearrange("p a t -> p (a t)")
                w = [t2_c2[g]]
                if g >= 4:
                    w.append(("A", th_c2[g - 4]))
                c = p.op("vector", lambda e, zv=zv, yv=yv, tv=tv: e.tensor_mul(
                    zv, yv, tv), waits=w)
                z_c2[g] = z_c2[g - 1] = c

            def st_tanh(g):
                if g % 2 == 0:
                    return
                tsl = (g - 1) % 4
                zv = zs[:, tsl:tsl + 2, :].rearrange("p a t -> p (a t)")
                hv = ths[:, tsl:tsl + 2, :].rearrange("p a t -> p (a t)")
                w = [("V", z_c2[g])]
                if g >= 4:
                    w += [w_c2[g - 4], ("V", h1bf_c[g - 4])]
                c = p.op("scalar", lambda e, hv=hv, zv=zv: e.activation(
                    hv, zv, AF.Tanh), waits=w)
                th_c2[g] = th_c2[g - 1] = c

            def st_w(g):
                if g % 2 == 0:
                    return
                tsl = (g - 1) % 4
                hv = ths[:, tsl:tsl + 2, :].rearrange("p a t -> p (a t)")
                c = p.op("gpsimd", lambda e, hv=hv: e.tensor_scalar(
                    hv, hv, 0.5, 0.5, ALU.mult, ALU.add),
                    waits=[("A", th_c2[g]), ("G", p.cnt["G"])], inc="G")
                w_c2[g] = w_c2[g - 1] = ("G", c)

            def st_h1bf(g):
                if g % 2 == 0:
                    return
                ph, m, jl, jg, M2 = gdec(g)
                ysl, tsl, jl0 = (g - 1) % 8, (g - 1) % 4, jl - 1
                yv = ys[:, ysl:ysl + 2, :].rearrange("p a t -> p (a t)")
                hv = ths[:, tsl:tsl + 2, :].rearrange("p a t -> p (a t)")
                h1v = h1_sb[:, jl0:jl0 + 2, :].rearrange("p a t -> p (a t)")
                w = [w_c2[g]]
                if M2 >= 1:
                    w.append(("P", o2mm_c[(M2 - 1, jl, ND5 - 1)]))
                c = p.op("vector", lambda e, h1v=h1v, yv=yv, hv=hv: e.tensor_mul(
                    h1v, yv, hv), waits=w)
                h1bf_c[g] = h1bf_c[g - 1] = c

            def st_out2(g):
                ph, m, jl, jg, M2 = gdec(g)
                for nch in range(ND5):
                    w = []
                    if nch == 0:
                        w = [w2c[(ph, jl)], ("V", h1bf_c[g])]
                        if jl == 0 and M2 >= 1:
                            w.append(("V", fin_c[M2 - 1]))
                    o2mm_c[(M2, jl, nch)] = p.op("tensor",
                        lambda e, jl=jl, nch=nch: e.matmul(
                            ps_y[nch][:, 0:512],
                            h1_sb[:, jl, 0:MT_B],
                            W2_sb[:, jl, nch * 512:(nch + 1) * 512],
                            start=(jl == 0), stop=(jl == FTH - 1)), waits=w, inc="P")
                if ph == 0 and m == NMS - 1:
                    gs = f"DW{jl % 2}"
                    c = p.op("gpsimd", lambda e, jl=jl: e.dma_start(
                        out=W2_sb[:, jl], in_=w2_h[:, FTH + jl]),
                        waits=[wch[gs], ("P", o2mm_c[(M2, jl, ND5 - 1)])],
                        inc=gs, n=16)
                    wch[gs] = (gs, c)
                    w2c[(1, jl)] = (gs, c)
                if jl == FTH - 1:
                    if ph == 0:
                        emit_th0(m)
                    else:
                        emit_upd(m)

            def emit_th0(m):
                last = 0
                for nch in range(ND5):
                    w = [("P", o2mm_c[(m, FTH - 1, nch)])]
                    last = p.op("vector", lambda e, m=m, nch=nch: e.tensor_copy(
                        th0_sb[:, m, nch * 512:(nch + 1) * 512],
                        ps_y[nch][:, 0:512]), waits=w)
                fin_c[m] = last

            def emit_upd(m):
                sl = m % 2
                M2 = NMS + m
                for nch in range(ND5):
                    w = [("P", o2mm_c[(M2, FTH - 1, nch)]), ("DC", cb2)]
                    if m >= 2:
                        w.append(sa_done[m - 2])
                    p.op("vector", lambda e, nch=nch, sl=sl: e.tensor_add(
                        ostB[:, sl, nch * 512:(nch + 1) * 512],
                        ps_y[nch][:, 0:512],
                        b2_sb[:, nch * 512:(nch + 1) * 512]), waits=w)
                    t1_c[(m, nch)] = p.op("vector",
                        lambda e, m=m, nch=nch, sl=sl: e.tensor_add(
                            ostB[:, sl, nch * 512:(nch + 1) * 512],
                            ostB[:, sl, nch * 512:(nch + 1) * 512],
                            th0_sb[:, m, nch * 512:(nch + 1) * 512]))
                gm_c[m] = p.op("vector", lambda e, m=m, sl=sl: e.tensor_single_scalar(
                    ostB[:, sl, :], ostB[:, sl, :], comb2[:, m, 3:4], ALU.mult))
                fin_c[M2] = gm_c[m]
                wsa = [("V", gm_c[m])] + [bw_done[t_] for t_ in
                                          (NM_B - 2, NM_B - 1) if t_ in bw_done]
                assert len(bw_done) == NM_B, "scatter before base0 fully emitted"
                csa = p.op("gpsimd", lambda e, m=m, sl=sl: e.indirect_dma_start(
                    out=out_hbm.rearrange("r s d -> (r s) d"),
                    out_offset=IOA(idxu[:, m:m + 1], 0),
                    in_=ostB[:, sl, :], in_offset=None, compute_op=ALU.add,
                    element_offset=r * S * D),
                    waits=wsa, inc=f"SA{m % 2}", n=16)
                sa_done[m] = (f"SA{m % 2}", csa)

            if r == 0:
                emit_xtile(0)
                emit_xtile(1)
                for _ti in range(2, NM_B):
                    drain0(4)
                    emit_xtile(_ti)
                drain0(len(q0s))
            prep(0)
            if NMS > 1:
                prep(1)
            if r > 0:
                emit_xtile(0)
                emit_xtile(1)
            xt_next = [NM_B if r == 0 else 2]
            XT_EVERY = max(1, (GSEQH - FTH) // max(1, NM_B - 2))
            LAG_O2 = min(12, max(7, FTH + 4))
            stages = ((0, st_h1mm), (1, st_yev), (2, st_sq), (3, st_z),
                      (4, st_tanh), (5, st_w), (6, st_h1bf), (LAG_O2, st_out2))
            for it in range(GSEQ2 + LAG_O2):
                if it % FTH == 0 and it // FTH + 2 < NMS:
                    prep(it // FTH + 2)
                if it % XT_EVERY == XT_EVERY - 1 and xt_next[0] < NM_B:
                    emit_xtile(xt_next[0])
                    xt_next[0] += 1
                for lag, fn in stages:
                    g = it - lag
                    if 0 <= g < GSEQ2:
                        fn(g)
            while xt_next[0] < NM_B:
                emit_xtile(xt_next[0])
                xt_next[0] += 1
            cBW = [("BW0", p.cnt["BW0"]), ("BW1", p.cnt["BW1"]),
                   ("SA0", p.cnt["SA0"]), ("SA1", p.cnt["SA1"])]
        wes.close()
        barB = p.snap()

        with contextlib.ExitStack() as ces:
            csb = lambda nm, shp, dt=F32: ces.enter_context(
                nc.sbuf_tensor(f"{nm}_r{r}", shp, dt))
            q_T = csb("q_T", [128, DT, S], BF16)
            k_T = csb("k_T", [128, DT, S], BF16)
            v_aug = csb("v_aug", [128, ST, H * 66], BF16)
            o_sb = csb("o_sb", [64, H, QCH], BF16)

            with contextlib.ExitStack() as c2:
                c2sb = lambda nm, shp, dt=F32: c2.enter_context(
                    nc.sbuf_tensor(f"{nm}_r{r}", shp, dt))
                bv_sb = c2sb("bv_sb", [128, D])
                wbuf = [c2sb("wbuf0", [128, DT, D], BF16),
                        c2sb("wbuf1", [128, DT, D], BF16)]
                ps_c4 = c2.enter_context(nc.psum_tensor(f"ps_c4_r{r}", [128, 2048]))
                wB = FB(barB)
                c0 = p.op("sync", lambda e: e.dma_start(out=bv_sb[:, :], in_=bv_h),
                     waits=wB, inc="XA0", n=16)
                cDS_bias = [("XA0", c0)]
                cw_qk = {}
                for kv in range(DT):
                    c = p.op("gpsimd", lambda e, kv=kv: e.dma_start(
                        out=wbuf[0][:, kv, :],
                        in_=wqT_h.rearrange("(k p) d -> p k d", p=128)[:, kv]),
                        waits=(wB if kv == 0 else [cw_qk[kv - 1]]),
                        inc="WQ", n=16)
                    cw_qk[kv] = ("WQ", c)
                cw_q = cw_qk[DT - 1]
                vv2 = v_aug[:, :, :].rearrange("p t (g wc) -> p t g wc", wc=132)
                p.op("vector", lambda e: e.memset(vv2[:, :, :, 64:65], 1.0),
                     waits=FB(barB))
                p.op("vector", lambda e: e.memset(vv2[:, :, :, 65:66], 0.0))
                p.op("vector", lambda e: e.memset(vv2[:, :, :, 130:131], 0.0))
                cV_ones = p.op("vector", lambda e: e.memset(
                    vv2[:, :, :, 131:132], 1.0))

                NS5 = S // 512
                mmq_c, evq_c = {}, {}

                def emit_proj(dst, wslot, bias, wready, tag, prev_tag):
                    for d in range(DT):
                        for n in range(NS5):
                            seq = d * NS5 + n
                            sl = seq % 4
                            for k in range(DT):
                                w = []
                                if seq == 0:
                                    w.append(wready[k] if isinstance(wready, dict)
                                             else wready)
                                if k == 0:
                                    if False:
                                        pass
                                    if seq >= 4:
                                        w.append(evq_c[(tag, seq - 4 +
                                                        (1 if NS5 >= 2 else 0))])
                                    elif prev_tag is not None:
                                        w.append(evq_c[(prev_tag, DT * NS5 - 1)])
                                        if NS5 >= 2:
                                            w.append(evq_c[(prev_tag,
                                                            DT * NS5 - 3)])
                                    else:
                                        w += [("V", barB["V"]), ("A", barB["A"])]
                                mmq_c[(tag, seq)] = p.op("tensor",
                                    lambda e, d=d, n=n, k=k, sl=sl, ws_=wslot: e.matmul(
                                        ps_c4[:, sl * 512:(sl + 1) * 512],
                                        wbuf[ws_][:, k, d * 128:(d + 1) * 128],
                                        xn_T[:, k, n * 512:(n + 1) * 512],
                                        start=(k == 0), stop=(k == DT - 1)),
                                    waits=w, inc="P")
                            if NS5 < 2:
                                evq_c[(tag, seq)] = ("A", p.op("scalar",
                                    lambda e, d=d, n=n, sl=sl, dst=dst, bias=bias:
                                    e.activation(
                                        dst[:, d, n * 512:(n + 1) * 512],
                                        ps_c4[:, sl * 512:(sl + 1) * 512],
                                        AF.Identity, bias=bias[:, d:d + 1]),
                                    waits=[("P", mmq_c[(tag, seq)])]))
                                continue
                            if seq % 2 == 1:
                                sl0, n0 = sl - 1, n - 1
                                dv = dst[:, d, n0 * 512:(n0 + 2) * 512]
                                pv = ps_c4[:, sl0 * 512:(sl0 + 2) * 512]
                                if (seq // 2) % 2 == 0:
                                    c = ("A", p.op("scalar",
                                        lambda e, dv=dv, pv=pv, d=d, bias=bias:
                                        e.activation(dv, pv, AF.Identity,
                                                     bias=bias[:, d:d + 1]),
                                        waits=[("P", mmq_c[(tag, seq)])]))
                                else:
                                    c = ("V", p.op("vector",
                                        lambda e, dv=dv, pv=pv, d=d, bias=bias:
                                        e.tensor_single_scalar(
                                            dv, pv, bias[:, d:d + 1], ALU.add),
                                        waits=[("P", mmq_c[(tag, seq)])]))
                                evq_c[(tag, seq)] = evq_c[(tag, seq - 1)] = c

                cw_kk = {}
                for kv in range(DT):
                    c = p.op("gpsimd", lambda e, kv=kv: e.dma_start(
                        out=wbuf[1][:, kv, :],
                        in_=wkT_h.rearrange("(k p) d -> p k d", p=128)[:, kv]),
                        waits=(wB if kv == 0 else [cw_kk[kv - 1]]),
                        inc="WK", n=16)
                    cw_kk[kv] = ("WK", c)
                cw_k = cw_kk[DT - 1]
                emit_proj(q_T, 0, bq_sb, cw_qk, "q", None)
                emit_proj(k_T, 1, bk_sb, cw_kk, "k", "q")
                cw_vk = {}
                for kv in range(DT):
                    c = p.op("gpsimd", lambda e, kv=kv: e.dma_start(
                        out=wbuf[0][:, kv, :],
                        in_=wvT_h.rearrange("(k p) d -> p k d", p=128)[:, kv]),
                        waits=([("P", mmq_c[("q", DT * NS5 - 1)])]
                               if kv == 0 else [cw_vk[kv - 1]]),
                        inc="WV", n=16)
                    cw_vk[kv] = ("WV", c)
                cw_v = cw_vk[DT - 1]

                mmv_c, evv_c = {}, {}
                for mt in range(ST):
                    for n in range(ND5):
                        seq = mt * ND5 + n
                        sl = seq % 4
                        for k in range(DT):
                            w = []
                            if seq == 0:
                                w.append(cw_vk[k])
                            if k == 0:
                                if seq >= 4:
                                    w.append(("V", evv_c[seq - 4 + 1]))
                                elif seq < 4:
                                    w.append(evq_c[("k", DT * NS5 - 1)])
                                    if NS5 >= 2:
                                        w.append(evq_c[("k", DT * NS5 - 3)])
                            mmv_c[seq] = p.op("tensor",
                                lambda e, mt=mt, n=n, k=k, sl=sl: e.matmul(
                                    ps_c4[:, sl * 512:(sl + 1) * 512],
                                    xn_T[:, k, mt * 128:(mt + 1) * 128],
                                    wbuf[0][:, k, n * 512:(n + 1) * 512],
                                    start=(k == 0), stop=(k == DT - 1)), waits=w, inc="P")
                        if seq % 2 == 1:
                            sl0, n0 = sl - 1, n - 1
                            dstv = v_aug[:, mt].rearrange(
                                "p (h c) -> p h c", h=H)[
                                :, n0 * 8:(n0 + 2) * 8, 0:64]
                            srcv = ps_c4[:, sl0 * 512:(sl0 + 2) * 512]
                            evv_c[seq] = evv_c[seq - 1] = p.op("vector",
                                lambda e, dstv=dstv, srcv=srcv, n0=n0: e.tensor_add(
                                    dstv, srcv.rearrange("p (h c) -> p h c", h=16),
                                    bv_sb[:, n0 * 512:(n0 + 2) * 512].rearrange(
                                        "p (h c) -> p h c", h=16)),
                                waits=[("P", mmv_c[seq])] + cDS_bias +
                                      [("V", cV_ones)])
                cA_C = p.cnt["A"]
                cV_C = p.cnt["V"]
            barC = p.snap()
            xnT_es.close()

            with contextlib.ExitStack() as des:
                dsb = lambda nm, shp, dt=F32: des.enter_context(
                    nc.sbuf_tensor(f"{nm}_r{r}", shp, dt, side="right"))
                attn_sb = dsb("attn_sb", [128, 6, 2, QCH], BF16)
                rs32 = dsb("rs32", [128, QCH])
                rsT = dsb("rsT", [128, QCH])
                rhi_sb = dsb("rhi_sb", [128, QCH], BF16)
                rlo_sb = dsb("rlo_sb", [128, QCH], BF16)
                o_un_t = dsb("o_un", [64, 2, QCH], BF16)
                woT_sb = dsb("woT_sb", [128, H // 2, D], BF16)
                o2_sb = dsb("o2_sb", [128, H // 2, QCH], BF16)
                bo_sb = dsb("bo_sb", [128, D])
                ostE = dsb("ostE", [128, 2, 512])
                bstE = dsb("bstE", [128, 2, 2, 512])
                ps_sr = [des.enter_context(
                    nc.psum_tensor(f"ps_s{k}_r{r}", [128, 2 * QCH]))
                    for k in range(2)]
                ps_o = [des.enter_context(nc.psum_tensor(f"ps_o{k}_r{r}", [128, QCH]))
                        for k in range(4)]

                cbo = p.op("sync", lambda e: e.dma_start(out=bo_sb[:, :], in_=bo_h),
                           waits=FB(barC), inc="BO", n=16)
                p.op("vector", lambda e: e.memset(rs32[64:96, 0:QCH], 1.0),
                     waits=[("P", barC["P"])])
                c = p.op("gpsimd", lambda e: e.dma_start(
                    out=woT_sb[:, :, :], in_=woTr_h), waits=FB(barC), inc="WO", n=16)
                cw_o = ("WO", c)
                ev_done = {}
                ow_done = {}
                prev_ev_last = [0, 0]
                prev_av_last = [0]
                prev_E_last = [0]

                o_un = o_un_t
                prev_outer_last = [0]
                for qh in range(NQ):
                    sc_c, ex_c, av_c, mul_c = {}, {}, {}, {}
                    hilo_c, outer_c, oun_c = {}, {}, {}

                    def emit_scpair(t, pi):
                        h0, h1 = 2 * pi, 2 * pi + 1
                        psd = ps_sr[t % 2]
                        w = []
                        if t >= 2:
                            w = [("A", ex_c[(pi, t - 2)])]
                        elif pi > 0:
                            w = [("A", ex_c[(pi - 1, ST - 2 + t)])]
                        elif qh > 0:
                            w = [("V", prev_ev_last[t % 2])]
                        else:
                            w = FB(barC)
                        for par, h in ((0, h0), (1, h1)):
                            for c in range(NC5):
                                sc_c[(h, t, c)] = p.op("tensor",
                                    lambda e, h=h, t=t, c=c, hd=h // 2, psd=psd,
                                    koff=64 * (h % 2), qh=qh: e.matmul(
                                        psd[:, (h % 2) * QCH + c * 512:
                                            (h % 2) * QCH + (c + 1) * 512],
                                        k_T[koff:koff + 64, hd, t * 128:(t + 1) * 128],
                                        q_T[koff:koff + 64, hd,
                                            qh * QCH + c * 512:qh * QCH + (c + 1) * 512],
                                        start=True, stop=True), waits=w, inc="P")
                                w = []
                        wex = [("P", sc_c[(h1, t, NC5 - 1)])]
                        if t >= 6:
                            wex.append(("P", av_c[(h1, t - 6, NC5 - 1)]))
                        elif pi > 0:
                            tp = t + 6 * ((ST - 1 - t) // 6)
                            wex.append(("P", av_c[(h1 - 2, tp, NC5 - 1)]))
                        elif qh > 0:
                            wex.append(("P", prev_av_last[0]))
                        else:
                            wex += [("A", barC["A"]), ("V", barC["V"])]
                        ex_c[(pi, t)] = p.op("scalar",
                            lambda e, t=t, psd=psd, ar=t % 6: e.activation(
                                attn_sb[:, ar, 0:2, 0:QCH].rearrange(
                                    "p a q -> p (a q)"),
                                psd[:, 0:2 * QCH], AF.Exp, scale=0.125), waits=wex)

                    def emit_av4(t, pi):
                        g = 2 * (pi % 2)
                        for par, h in ((0, 2 * pi), (1, 2 * pi + 1)):
                            w = [("A", ex_c[(pi, t)])]
                            if t == 0:
                                if pi > 1:
                                    w += [("V", mul_c[(2 * pi - 4, qh)]),
                                          ("V", mul_c[(2 * pi - 3, qh)])]
                                elif qh > 0:
                                    w.append(("V", mul_prev[pi % 2]))
                                else:
                                    w.append(("V", cV_C))
                            for c in range(NC5):
                                av_c[(h, t, c)] = p.op("tensor",
                                    lambda e, h=h, t=t, c=c, g=g, ar=t % 6: e.matmul(
                                        ps_o[g + h % 2][0:66, c * 512:(c + 1) * 512],
                                        v_aug[:, t, h * 66:(h + 1) * 66],
                                        attn_sb[:, ar, h % 2, c * 512:(c + 1) * 512],
                                        start=(t == 0), stop=(t == ST - 1)),
                                    waits=w, inc="P")
                                w = []

                    def emit_outer(hh):
                        tgt = 2 * ((hh // 2) % 2) + (1 - hh % 2)
                        last = 0
                        for c in range(NC5):
                            w = [("V", oun_c[hh // 2])] if c == 0 else []
                            p.op("tensor", lambda e, c=c, hh=hh, tgt=tgt: e.matmul(
                                ps_o[tgt][0:64, c * 512:(c + 1) * 512],
                                pick_sb[64:66, (hh % 2) * 64:(hh % 2) * 64 + 64],
                                rhi_sb[64:66, c * 512:(c + 1) * 512],
                                start=True, stop=False), waits=w, inc="P")
                            last = p.op("tensor", lambda e, c=c, hh=hh, tgt=tgt:
                                e.matmul(
                                    ps_o[tgt][0:64, c * 512:(c + 1) * 512],
                                    pick_sb[64:66, (hh % 2) * 64:(hh % 2) * 64 + 64],
                                    rlo_sb[64:66, c * 512:(c + 1) * 512],
                                    start=False, stop=True), inc="P")
                        outer_c[hh] = last

                    def emit_norm_tail(hh):
                        tgt = 2 * ((hh // 2) % 2) + (1 - hh % 2)
                        mul_c[(hh, qh)] = p.op("vector", lambda e, hh=hh, tgt=tgt:
                            e.tensor_mul(
                                o_sb[0:64, hh, 0:QCH], o_un[0:64, hh % 2, 0:QCH],
                                ps_o[tgt][0:64, 0:QCH]),
                            waits=[("P", outer_c[hh])])

                    def emit_dance(pi):
                        h0, h1 = 2 * pi, 2 * pi + 1
                        g = 2 * (pi % 2)
                        wo_ = [("P", av_c[(h1, ST - 1, NC5 - 1)])]
                        if pi == 0 and qh > 0:
                            wo_.append(("P", prev_E_last[0]))
                        p.op("vector", lambda e, g=g: e.tensor_copy(
                            rs32[64:66, 0:QCH], ps_o[g + 1][64:66, 0:QCH]), waits=wo_)
                        p.op("vector", lambda e, g=g: e.tensor_copy(
                            rs32[64:65, 0:QCH], ps_o[g][64:65, 0:QCH]),
                            waits=[("P", av_c[(h0, ST - 1, NC5 - 1)])])
                        p.op("vector", lambda e: e.transpose(
                            rsT[64:96, 0:QCH], rs32[64:96, 0:QCH]))
                        p.op("vector", lambda e: e.reciprocal(
                            rsT[64:96, 0:QCH].rearrange(
                                "p (c j) -> p c j", j=32)[:, :, 0:2],
                            rsT[64:96, 0:QCH].rearrange(
                                "p (c j) -> p c j", j=32)[:, :, 0:2]))
                        p.op("vector", lambda e: e.transpose(
                            rs32[64:96, 0:QCH], rsT[64:96, 0:QCH]))
                        p.op("vector", lambda e: e.tensor_copy(
                            rhi_sb[64:66, 0:QCH], rs32[64:66, 0:QCH]),
                            waits=[("P", prev_outer_last[0])])
                        hilo_c[pi] = p.op("vector", lambda e: e.tensor_sub(
                            rlo_sb[64:66, 0:QCH], rs32[64:66, 0:QCH],
                            rhi_sb[64:66, 0:QCH]))
                        p.op("vector", lambda e, g=g: e.tensor_copy(
                            o_un[0:64, 0, 0:QCH], ps_o[g][0:64, 0:QCH]))
                        oun_c[pi] = p.op("vector", lambda e, g=g: e.tensor_copy(
                            o_un[0:64, 1, 0:QCH], ps_o[g + 1][0:64, 0:QCH]))

                    OUT_T = min(7, ST - 1)
                    DAN_T = min(2, ST - 1)
                    due, tail_avs = {}, []
                    for ta in range(ST):
                        e_s = ta + 3
                        if e_s <= ST - 1:
                            due.setdefault(e_s, []).append(ta)
                        else:
                            tail_avs.append(ta)
                    for pi in range(H // 2):
                        for t in range(ST):
                            for ta in due.get(t, []):
                                emit_av4(ta, pi)
                            emit_scpair(t, pi)
                            if t == 0 and pi > 0 and ST >= 16:
                                for ta in tail_avs:
                                    emit_av4(ta, pi - 1)
                            if t == DAN_T and pi > 0:
                                emit_dance(pi - 1)
                            if t == DAN_T:
                                drain(2)
                            if t == OUT_T and pi > 0:
                                emit_outer(2 * pi - 1)
                                emit_norm_tail(2 * pi - 1)
                                emit_outer(2 * pi - 2)
                                emit_norm_tail(2 * pi - 2)
                                prev_outer_last[0] = outer_c[2 * pi - 2]
                        if ST < 16:
                            for ta in tail_avs:
                                emit_av4(ta, pi)
                    if ST >= 16:
                        for ta in tail_avs:
                            emit_av4(ta, H // 2 - 1)
                    emit_dance(H // 2 - 1)
                    NMq = QCH // 128
                    EROT = 2 if QCH >= 1024 else 1

                    def psE(mt, nch):
                        s4 = (mt * ND5 + nch) % 4
                        return ps_sr[s4 // 2][:, (s4 % 2) * 512:(s4 % 2 + 1) * 512]

                    evE = {}

                    def e_read(mt):
                        ti = qh * NMq + mt
                        wbr = list(cBW)
                        if qh == 0 and mt < 2:
                            wbr += FB(barC)
                        if ti >= 2:
                            wbr.append(("V", ev_done.get(ti - 2, 0)))
                        c = p.op("sync", lambda e, ti=ti, sl=mt % 2:
                            e.dma_start(
                                out=bstE[:, sl, :, :].rearrange(
                                    "p a c -> p (a c)"),
                                in_=out_hbm[r, ti * 128:(ti + 1) * 128, :]),
                            waits=wbr, inc=f"BR{mt % 2}", n=16)
                        return [(f"BR{mt % 2}", c)]

                    def e_mms(mt, hps, extra):
                        last = 0
                        ti = qh * NMq + mt
                        for nch in range(ND5):
                            for hp in hps:
                                w = []
                                if hp == 0:
                                    w = list(extra) + [cw_o,
                                         ("A", ex_c[(H // 2 - 1, ST - 1)])]
                                    if nch == 0 and ti >= 2:
                                        w.append(("V", ev_done.get(ti - 2, 0)))
                                elif hp == H // 2 - 1:
                                    w = list(cw_oc[0])
                                last = p.op("tensor",
                                    lambda e, mt=mt, nch=nch, hp=hp: e.matmul(
                                        ps_sr[mt % 2][:, nch * 512:(nch + 1) * 512],
                                        o2_sb[:, hp, mt * 128:(mt + 1) * 128],
                                        woT_sb[:, hp, nch * 512:(nch + 1) * 512],
                                        start=(hp == 0), stop=(hp == H // 2 - 1)),
                                    waits=w, inc="P")
                        return last

                    def e_tail(mt, dr_cs, lastmm):
                        ti = qh * NMq + mt
                        wev = [("P", lastmm), ("BO", cbo)]
                        if ti >= 1 and (ti - 1) in ow_done:
                            wev.append(ow_done[ti - 1])
                        p.op("vector", lambda e, mt=mt: e.tensor_add(
                            ostE[:, :, :].rearrange("p a c -> p (a c)"),
                            ps_sr[mt % 2][:, 0:2 * 512],
                            bo_sb[:, 0:1024]), waits=wev)
                        evE[mt] = p.op("vector", lambda e, sl=mt % 2:
                            e.tensor_add(
                                ostE[:, :, :].rearrange("p a c -> p (a c)"),
                                ostE[:, :, :].rearrange("p a c -> p (a c)"),
                                bstE[:, sl, :, :].rearrange("p a c -> p (a c)")),
                            waits=dr_cs)
                        ev_done[ti] = evE[mt]
                        c = p.op("sync", lambda e, ti=ti: e.dma_start(
                            out=out_hbm[r, ti * 128:(ti + 1) * 128, :],
                            in_=ostE[:, :, :].rearrange("p a c -> p (a c)")),
                            waits=[("V", evE[mt])], inc="OW0", n=16)
                        ow_done[ti] = ("OW0", c)
                        prev_E_last[0] = lastmm

                    osp = o_sb[:, :, 0:QCH].rearrange(
                        "p (h two) q -> p h two q", two=2)
                    HP = H // 2
                    woc = [("V", mul_c[(H - 4, qh)]),
                           ("V", mul_c[(H - 3, qh)]),
                           ("P", prev_E_last[0])]
                    oa0 = p.op("sync", lambda e, osp=osp, HP=HP: e.dma_start(
                        out=o2_sb[0:64, 0:HP - 1, :],
                        in_=osp[:, 0:HP - 1, 0, :]),
                        waits=woc, inc="OC0", n=16)
                    oa1 = p.op("sync", lambda e, osp=osp, HP=HP: e.dma_start(
                        out=o2_sb[64:128, 0:HP - 1, :],
                        in_=osp[:, 0:HP - 1, 1, :]),
                        waits=woc, inc="OC1", n=16)
                    cw_oa = [("OC0", oa0), ("OC1", oa1)]
                    e_dr01 = {0: e_read(0), 1: e_read(1)}
                    e_lm01 = {mt: e_mms(mt, list(range(HP - 1)), cw_oa)
                              for mt in range(2)}
                    emit_outer(H - 1)
                    emit_norm_tail(H - 1)
                    emit_outer(H - 2)
                    emit_norm_tail(H - 2)
                    prev_outer_last[0] = outer_c[H - 2]
                    oc0 = p.op("sync", lambda e, osp=osp, HP=HP: e.dma_start(
                        out=o2_sb[0:64, HP - 1, :], in_=osp[:, HP - 1, 0, :]),
                        waits=[("V", mul_c[(H - 2, qh)]), ("OC0", oa0)],
                        inc="OC0", n=16)
                    oc1 = p.op("sync", lambda e, osp=osp, HP=HP: e.dma_start(
                        out=o2_sb[64:128, HP - 1, :], in_=osp[:, HP - 1, 1, :]),
                        waits=[("V", mul_c[(H - 1, qh)]), ("OC1", oa1)],
                        inc="OC1", n=16)
                    cw_oc = [[("OC0", oc0), ("OC1", oc1)]]

                    mul_prev = {0: mul_c[(H - 4, qh)], 1: mul_c[(H - 2, qh)]}
                    prev_av_last[0] = av_c[(H - 1, ST - 1, NC5 - 1)]

                    if qh == NQ - 1:
                        rowtail[r] = {"P": p.cnt["P"], "V": p.cnt["V"]}

                    for mt in range(NMq):
                        drain(1)
                        if mt < 2:
                            lastmm = e_mms(mt, [H // 2 - 1], [])
                            e_tail(mt, e_dr01[mt], lastmm)
                        else:
                            dr_cs = e_read(mt)
                            lastmm = e_mms(mt, list(range(H // 2)), cw_oa)
                            e_tail(mt, dr_cs, lastmm)
                    prev_ev_last[(NMq - 2) % 2] = evE[NMq - 2]
                    prev_ev_last[(NMq - 1) % 2] = evE[NMq - 1]
        drain(len(nxt_qs))
        row_es.close()

    for r in range(R):
        emit_row(r)

    p.op("sync", lambda e: e.nop(),
         waits=[(k, p.cnt[k]) for k in ("OW0", "OW1", "BW0", "BW1", "BR0", "BR1")])
    p.finish()
    p.est.close()
    nc.finalize()
    return nc


def _pick_mat():
    pm = np.zeros((128, 128), dtype=ml_dtypes.bfloat16)
    pm[64, 0:64] = 1
    pm[65, 64:128] = 1
    return pm


def _host_layouts(inputs, n_cores):
    x = np.ascontiguousarray(inputs["L0_attn_x"], dtype=np.float32)
    wq = inputs["L0_attn_wq"]; wk = inputs["L0_attn_wk"]
    wv = inputs["L0_attn_wv"]; wo = inputs["L0_attn_wo"]
    W1 = inputs["L0_ffn_W1"]; W2 = inputs["L0_ffn_W2"]
    D = wq.shape[0]; F = W1.shape[1]
    DT, FT, H = D // 128, F // 128, D // 64
    f32 = lambda a: np.ascontiguousarray(a, np.float32)
    com = {
        "wqT": f32(wq.T),
        "wkT": f32(wk.T),
        "wvT": f32(wv.T),
        "woTr": f32(np.asarray(wo).T.reshape(H // 2, 2, 64, D)
                    .transpose(1, 2, 0, 3).reshape(128, H // 2, D)),
        "w1r": f32(np.asarray(W1).reshape(DT, 128, FT, 128).transpose(1, 2, 0, 3)),
        "w2r": f32(np.asarray(W2).reshape(FT, 128, D).transpose(1, 0, 2)),
        "bqr": f32(np.asarray(inputs["L0_attn_bq"]).reshape(DT, 128).T),
        "bkr": f32(np.asarray(inputs["L0_attn_bk"]).reshape(DT, 128).T),
        "b1r": f32(np.asarray(inputs["L0_ffn_b1"]).reshape(FT, 128).T),
        "bvrep": f32(np.broadcast_to(inputs["L0_attn_bv"], (128, D))),
        "borep": f32(np.broadcast_to(inputs["L0_attn_bo"], (128, D))),
        "b2rep": f32(np.broadcast_to(inputs["L0_ffn_b2"], (128, D))),
        "wrrep": f32(np.broadcast_to(inputs["L0_ffn_w_router"], (128, D))),
        "ident": np.eye(128, dtype=ml_dtypes.bfloat16),
        "pick": _pick_mat(),
    }
    B = x.shape[0]
    R = B // n_cores
    return [dict(com, x=np.ascontiguousarray(x[c * R:(c + 1) * R]))
            for c in range(n_cores)]


def kernel(**inputs):
    x = inputs["L0_attn_x"]
    B, S, D = x.shape
    F = inputs["L0_ffn_W1"].shape[1]
    n_cores = 8
    nc = build_program(S, D, F, B // n_cores)
    in_maps = _host_layouts(inputs, n_cores)
    res = run_bass_kernel_spmd(nc, in_maps, core_ids=list(range(n_cores)))
    return np.concatenate([res.results[c]["out"] for c in range(n_cores)], axis=0)


# version 71
# speedup vs baseline: 1.0136x; 1.0136x over previous
import contextlib
import numpy as np
import ml_dtypes
import concourse.bass as bass
from concourse.bass import IndirectOffsetOnAxis as IOA
import concourse.mybir as mybir
from concourse.bass_utils import run_bass_kernel_spmd

F32 = mybir.dt.float32
BF16 = mybir.dt.bfloat16
U32 = mybir.dt.uint32
AF = mybir.ActivationFunctionType
ALU = mybir.AluOpType
AX = mybir.AxisListType

ENGS = ("sync", "gpsimd", "tensor", "scalar", "vector")

GA = 0.7978845608028654
GB = 0.044715
N_ITER = 32


class Prog:

    def __init__(self, nc):
        self.nc = nc
        self.streams = {e: [] for e in ENGS}
        self.gmap = {e: {} for e in ENGS}
        self.sems = {}
        self.cnt = {}
        self.est = contextlib.ExitStack()

    def sem(self, name):
        h = self.est.enter_context(self.nc.semaphore(name))
        self.sems[name] = h
        self.cnt[name] = 0

    def op(self, eng, fn, waits=(), inc=None, n=1):
        waits = list(waits)
        if eng == "vector":
            waits.append(("V", self.cnt["V"]))
            inc, n = "V", 1
        elif eng == "scalar":
            waits.append(("A", self.cnt["A"]))
            inc, n = "A", 1
        wd = {}
        for s, v in waits:
            if v > 0:
                wd[s] = max(wd.get(s, 0), v)
        g = self.gmap[eng]
        wl = [(s, v) for s, v in sorted(wd.items()) if v > g.get(s, 0)]
        g.update(wl)
        self.streams[eng].append((fn, wl, inc, n))
        if inc is not None:
            self.cnt[inc] += n
        return self.cnt[inc] if inc is not None else None

    def snap(self):
        return dict(self.cnt)

    def replay(self, name, eng):
        sems = self.sems
        for fn, waits, inc, n in self.streams[name]:
            for s, v in waits[:-1]:
                eng.wait_ge(sems[s], v)
            inst = fn(eng)
            if waits:
                sv = waits[-1]
                inst._wait_ge(sems[sv[0]], sv[1])
            if inc is not None:
                inst.then_inc(sems[inc], n)

    def finish(self):
        with self.nc.Block() as blk:
            @blk.sync
            def _(eng):
                self.replay("sync", eng)

            @blk.gpsimd
            def _(eng):
                self.replay("gpsimd", eng)

            @blk.tensor
            def _(eng):
                self.replay("tensor", eng)

            @blk.scalar
            def _(eng):
                self.replay("scalar", eng)

            @blk.vector
            def _(eng):
                self.replay("vector", eng)


def build_program(S, D, F, R, AROT_OVERRIDE=None, QCH_OVERRIDE=None):
    DK = 64
    H = D // DK
    ST = S // 128
    DT = D // 128
    FT = F // 128
    ND5 = D // 512
    QCH = min(512, S) if QCH_OVERRIDE is None else QCH_OVERRIDE
    NQ = S // QCH
    NC5 = QCH // 512
    MT_B = 128
    NM_B = S // MT_B
    NSUB = 1
    K_SEL = S // 2
    SEL = S // 2
    NMS = SEL // 128
    AROT = min(4, ST) if AROT_OVERRIDE is None else AROT_OVERRIDE

    nc = bass.Bass("TRN2", target_bir_lowering=False)
    p = Prog(nc)
    est = p.est

    def din(name, shape, dt=F32):
        return nc.dram_tensor(name, shape, dt, kind="ExternalInput").ap()

    x_hbm = din("x", [R, S, D])
    out_hbm = nc.dram_tensor("out", [R, S, D], F32, kind="ExternalOutput").ap()
    wqT_h = din("wqT", [D, D])
    wkT_h = din("wkT", [D, D])
    wvT_h = din("wvT", [D, D])
    woTr_h = din("woTr", [128, H // 2, D])
    w1_h = din("w1r", [128, FT, DT, 128])
    w2_h = din("w2r", [128, FT, D])
    bq_h = din("bqr", [128, DT])
    bk_h = din("bkr", [128, DT])
    b1_h = din("b1r", [128, FT])
    bv_h = din("bvrep", [128, D])
    bo_h = din("borep", [128, D])
    b2_h = din("b2rep", [128, D])
    wr_h = din("wrrep", [128, D])
    id_h = din("ident", [128, 128], BF16)
    combd = nc.dram_tensor("combd", [S, 4], F32, kind="Internal").ap()
    pick_h = din("pick", [128, 128], BF16)

    DMAS = ("DC", "WQ", "WK", "WV", "WO", "XA0", "XA1", "XB0", "XB1",
            "BW0", "BW1", "BR0", "BR1", "OW0", "OW1", "DW0", "DW1",
            "CS", "CR", "XG0", "XG1", "SA0", "SA1", "BO", "OC0", "OC1")
    for s in ("P", "A", "V", "G") + DMAS:
        p.sem(s)

    sb = lambda name, shape, dt=F32: est.enter_context(nc.sbuf_tensor(name, shape, dt))

    bq_sb = sb("bq_sb", [128, DT])
    bk_sb = sb("bk_sb", [128, DT])
    b1_sb = sb("b1_sb", [128, FT])
    id_sb = sb("id_sb", [128, 128], BF16)
    ones64 = sb("ones64", [128, 64], BF16)
    pick_sb = sb("pick_sb", [128, 128], BF16)
    ones32 = sb("ones32", [1, 32])
    ssum = sb("ssum", [128, 32])
    ssq = sb("ssq", [128, 32])
    m_sb = sb("m_sb", [128, 32])
    var_sb = sb("var_sb", [128, 32])
    r_sb = sb("r_sb", [128, 32])
    nmr_sb = sb("nmr_sb", [128, 32])
    r1p_sb = sb("r1p_sb", [128, 32])
    scores = sb("scores", [128, 32])
    g_sb = sb("g_sb", [128, 32])
    s16 = sb("s16", [32, 128])
    th16 = sb("th16", [32, 128])
    g16 = sb("g16", [32, 128])
    gcnt = sb("gcnt", [32, 128])
    scr_a = sb("scr_a", [32, 32])
    scr_b = sb("scr_b", [32, 32])
    cnt_scr = sb("cnt_scr", [32, 32])
    tpad = sb("tpad", [32, 32])
    tT = sb("tT", [32, 32])
    sm = sb("sm", [1, 8])
    scA = sb("scA", [32, 128])
    scB = sb("scB", [32, 128])
    iotaF = sb("iotaF", [32, 128])
    gpos32 = sb("gpos32", [32, 128])
    itmp = sb("itmp", [128, 32])
    tokT = sb("tokT", [128, 32])
    comb_sb = sb("comb_sb", [128, 32, 4])
    offs_u = sb("offs_u", [128, 32], U32)
    idxu = sb("idxu", [128, 32], U32)
    comb2 = sb("comb2", [128, NMS, 4])
    xstO = sb("xstO", [128, 2, D])
    scrO = sb("scrO", [128, 1, D])
    wr_sb = sb("wr_sb", [128, D])
    LO, HI, TC, MK, CC, D1, D2, NM = 0, 1, 2, 3, 4, 5, 6, 7

    for dst, src in ((bq_sb, bq_h), (bk_sb, bk_h), (b1_sb, b1_h),
                     (id_sb, id_h), (pick_sb, pick_h), (wr_sb, wr_h)):
        p.op("sync", lambda e, d=dst, s=src: e.dma_start(out=d[:, :], in_=s),
             inc="DC", n=16)
    cDC = p.cnt["DC"]
    p.op("vector", lambda e: e.memset(ones64[:, :], 1.0), waits=[("DC", cDC)])
    p.op("vector", lambda e: e.memset(ones32[:, :], 1.0))
    p.op("vector", lambda e: e.memset(scA[:, :], 1.0))
    _cur, _oth = scA, scB
    for _sh in (1, 2, 4, 8, 16, 32, 64):
        p.op("vector", lambda e, sh=_sh, c=_cur, o=_oth: e.tensor_copy(
            o[:, 0:sh], c[:, 0:sh]))
        p.op("vector", lambda e, sh=_sh, c=_cur, o=_oth: e.tensor_add(
            o[:, sh:128], c[:, sh:128], c[:, 0:128 - sh]))
        _cur, _oth = _oth, _cur
    p.op("vector", lambda e, c=_cur: e.tensor_scalar(
        iotaF[:, :], c[:, :], 1.0, -1.0, ALU.mult, ALU.add))
    p.op("vector", lambda e: e.transpose(tT[:, :], iotaF[0:32, 0:32]))
    p.op("vector", lambda e: e.tensor_scalar(
        tpad[:, :], tT[:, :], 128.0, None, ALU.mult))
    p.op("vector", lambda e: e.tensor_single_scalar(
        gpos32[:, :], iotaF[:, :], tpad[:, 0:1], ALU.add))
    for _b in range(4):
        p.op("vector", lambda e, b=_b: e.transpose(
            itmp[32 * b:32 * b + 32, 0:32], iotaF[0:32, 32 * b:32 * b + 32]))
    for _ti in range(S // 128):
        p.op("vector", lambda e, ti=_ti: e.tensor_scalar(
            tokT[:, ti:ti + 1], itmp[:, 0:1], 1.0, float(128 * ti),
            ALU.mult, ALU.add))
    p.op("vector", lambda e: e.tensor_copy(
        comb_sb[:, 0:S // 128, 0:1],
        tokT[:, 0:S // 128].rearrange("p (t c) -> p t c", c=1)))

    prev_cr = [0]
    aov = {}
    rowtail = {}

    def make_A_quanta(rr):
        FBq = lambda sn: [(k, sn[k]) for k in ("P", "A", "V") + DMAS]
        aq = {}
        w0 = lambda: FBq(aq["snap0"])
        qs = []
        add = qs.append
        st_c = {}

        def q_snap():
            aq["snap0"] = p.snap()
        add(q_snap)

        def q_stat(i):
            w = w0() if i < 2 else [("V", st_c[("sq", i - 2)])]
            xsem = f"XA{i % 2}"
            c = p.op("sync", lambda e, i=i, sl=i % 2: e.dma_start(
                out=xstO[:, sl, :], in_=x_hbm[rr, i * 128:(i + 1) * 128, :]),
                waits=w, inc=xsem, n=16)
            st_c[("red", i)] = p.op("vector", lambda e, i=i, sl=i % 2:
                e.tensor_reduce(ssum[:, i:i + 1], xstO[:, sl, :], AX.X, ALU.add),
                waits=[(xsem, c)] + (w0() if i == 0 else []))
            p.op("vector", lambda e, sl=i % 2: e.tensor_mul(
                scrO[:, 0, :], xstO[:, sl, :], xstO[:, sl, :]),
                waits=[(xsem, c)])
            st_c[("sq", i)] = p.op("vector", lambda e, i=i: e.tensor_reduce(
                ssq[:, i:i + 1], scrO[:, 0, :], AX.X, ALU.add))
        for _i in range(ST):
            add(lambda i=_i: q_stat(i))

        def q_fin():
            inv_d = 1.0 / D
            p.op("vector", lambda e: e.tensor_scalar(
                m_sb[:, 0:ST], ssum[:, 0:ST], inv_d, None, ALU.mult), waits=w0())
            p.op("vector", lambda e: e.tensor_scalar(
                var_sb[:, 0:ST], ssq[:, 0:ST], inv_d, None, ALU.mult),
                waits=[("V", st_c[("sq", ST - 1)])])
            p.op("vector", lambda e: e.tensor_mul(
                scrO[:, 0, 0:ST], m_sb[:, 0:ST], m_sb[:, 0:ST]))
            p.op("vector", lambda e: e.tensor_sub(
                var_sb[:, 0:ST], var_sb[:, 0:ST], scrO[:, 0, 0:ST]))
            cV = p.op("vector", lambda e: e.tensor_scalar(
                scrO[:, 0, 0:ST], var_sb[:, 0:ST], 1.0, 1e-5, ALU.mult, ALU.add))
            cA_sq = p.op("scalar", lambda e: e.activation(
                r_sb[:, 0:ST], scrO[:, 0, 0:ST], AF.Sqrt),
                waits=[("V", cV)])
            p.op("vector", lambda e: e.reciprocal(r_sb[:, 0:ST], r_sb[:, 0:ST]),
                 waits=[("A", cA_sq)])
            for _ in range(2):
                p.op("vector", lambda e: e.tensor_mul(
                    scrO[:, 0, ST:2 * ST], r_sb[:, 0:ST], r_sb[:, 0:ST]))
                p.op("vector", lambda e: e.tensor_mul(
                    scrO[:, 0, ST:2 * ST], scrO[:, 0, ST:2 * ST], scrO[:, 0, 0:ST]))
                p.op("vector", lambda e: e.tensor_scalar(
                    scrO[:, 0, ST:2 * ST], scrO[:, 0, ST:2 * ST], -0.5, 1.5,
                    ALU.mult, ALU.add))
                p.op("vector", lambda e: e.tensor_mul(
                    r_sb[:, 0:ST], r_sb[:, 0:ST], scrO[:, 0, ST:2 * ST]))
            p.op("vector", lambda e: e.tensor_mul(
                nmr_sb[:, 0:ST], m_sb[:, 0:ST], r_sb[:, 0:ST]))
            p.op("vector", lambda e: e.tensor_scalar(
                nmr_sb[:, 0:ST], nmr_sb[:, 0:ST], -1.0, None, ALU.mult))
            p.op("vector", lambda e: e.tensor_scalar(
                r1p_sb[:, 0:ST], r_sb[:, 0:ST], 1.0, None, ALU.add))
            aq["cV_fin"] = p.cnt["V"]
            aq["cA_fin"] = p.cnt["A"]
        add(q_fin)

        def q_router(i):
            w = [("V", aq["cV_fin"]), ("A", aq["cA_fin"])] if i < 2 else \
                [("V", st_c[("amr", i - 2)])]
            xsem = f"XA{i % 2}"
            c = p.op("sync", lambda e, i=i, sl=i % 2: e.dma_start(
                out=xstO[:, sl, :], in_=x_hbm[rr, i * 128:(i + 1) * 128, :]),
                waits=w, inc=xsem, n=16)
            p.op("vector", lambda e, i=i, sl=i % 2: e.tensor_scalar(
                scrO[:, 0, :], xstO[:, sl, :], r_sb[:, i:i + 1],
                nmr_sb[:, i:i + 1], ALU.mult, ALU.add),
                waits=[(xsem, c), ("DC", cDC)])
            p.op("vector", lambda e: e.tensor_mul(
                scrO[:, 0, :], scrO[:, 0, :], wr_sb[:, :]))
            st_c[("amr", i)] = p.op("vector", lambda e, i=i: e.tensor_reduce(
                scores[:, i:i + 1], scrO[:, 0, :], AX.X, ALU.add))
        for _i in range(ST):
            add(lambda i=_i: q_router(i))

        def q_bis_init():
            p.op("vector", lambda e: e.memset(scores[:, ST:32], -3e38))
            p.op("vector", lambda e: e.memset(cnt_scr[:, :], 0.0))
            p.op("vector", lambda e: e.memset(scr_a[:, :], -3e38))
            cV = p.op("vector", lambda e: e.memset(scr_b[:, :], 3e38))
            for bi in range(4):
                cV = p.op("vector", lambda e, bi=bi: e.transpose(
                    s16[:, bi * 32:(bi + 1) * 32],
                    scores[bi * 32:(bi + 1) * 32, 0:32]))
            aq["cA_th"] = p.op("scalar", lambda e: e.activation(
                th16[0:32, :], s16[0:32, :], AF.Tanh, scale=0.5),
                waits=[("V", cV)] + w0())
            p.op("vector", lambda e: e.tensor_reduce(
                scr_a[0:ST, 0:1], s16[0:ST, :], AX.X, ALU.max))
            p.op("vector", lambda e: e.tensor_reduce(
                scr_b[0:ST, 0:1], s16[0:ST, :], AX.X, ALU.min))
            p.op("vector", lambda e: e.transpose(tpad[:, :], scr_a[:, :]))
            p.op("vector", lambda e: e.tensor_reduce(
                sm[:, HI:HI + 1], tpad[0:1, 0:32], AX.X, ALU.max))
            p.op("vector", lambda e: e.transpose(tpad[:, :], scr_b[:, :]))
            p.op("vector", lambda e: e.tensor_reduce(
                sm[:, LO:LO + 1], tpad[0:1, 0:32], AX.X, ALU.min))
            p.op("vector", lambda e: e.tensor_scalar(
                sm[:, HI:HI + 1], sm[:, HI:HI + 1], 1.0, None, ALU.add))
            p.op("vector", lambda e: e.tensor_scalar(
                sm[:, LO:LO + 1], sm[:, LO:LO + 1], -1.0, None, ALU.add))
            p.op("vector", lambda e: e.tensor_add(
                sm[:, TC:TC + 1], sm[:, LO:LO + 1], sm[:, HI:HI + 1]))
            p.op("vector", lambda e: e.tensor_scalar(
                sm[:, TC:TC + 1], sm[:, TC:TC + 1], 0.5, None, ALU.mult))
        add(q_bis_init)

        def bcast_t():
            p.op("vector", lambda e: e.tensor_single_scalar(
                tpad[0:1, :], ones32[:, :], sm[:, TC:TC + 1], ALU.mult))
            p.op("vector", lambda e: e.transpose(tT[:, :], tpad[:, :]))

        def q_bis_iter():
            bcast_t()
            p.op("vector", lambda e: e.tensor_single_scalar(
                gcnt[0:ST, :], s16[0:ST, :], tT[0:ST, 0:1], ALU.is_gt))
            p.op("vector", lambda e: e.tensor_reduce(
                cnt_scr[0:ST, 0:1], gcnt[0:ST, :], AX.X, ALU.add))
            p.op("vector", lambda e: e.transpose(tpad[:, :], cnt_scr[:, :]))
            p.op("vector", lambda e: e.tensor_reduce(
                sm[:, CC:CC + 1], tpad[0:1, 0:32], AX.X, ALU.add))
            p.op("vector", lambda e: e.tensor_single_scalar(
                sm[:, MK:MK + 1], sm[:, CC:CC + 1], float(K_SEL), ALU.is_ge))
            p.op("vector", lambda e: e.tensor_scalar(
                sm[:, NM:NM + 1], sm[:, MK:MK + 1], -1.0, 1.0, ALU.mult, ALU.add))
            p.op("vector", lambda e: e.tensor_sub(
                sm[:, D1:D1 + 1], sm[:, TC:TC + 1], sm[:, LO:LO + 1]))
            p.op("vector", lambda e: e.scalar_tensor_tensor(
                sm[:, LO:LO + 1], sm[:, D1:D1 + 1], sm[:, MK:MK + 1],
                sm[:, LO:LO + 1], ALU.mult, ALU.add))
            p.op("vector", lambda e: e.tensor_sub(
                sm[:, D2:D2 + 1], sm[:, TC:TC + 1], sm[:, HI:HI + 1]))
            p.op("vector", lambda e: e.scalar_tensor_tensor(
                sm[:, HI:HI + 1], sm[:, D2:D2 + 1], sm[:, NM:NM + 1],
                sm[:, HI:HI + 1], ALU.mult, ALU.add))
            p.op("vector", lambda e: e.tensor_add(
                sm[:, TC:TC + 1], sm[:, LO:LO + 1], sm[:, HI:HI + 1]))
            p.op("vector", lambda e: e.tensor_scalar(
                sm[:, TC:TC + 1], sm[:, TC:TC + 1], 0.5, None, ALU.mult))
        for _ in range(N_ITER):
            add(q_bis_iter)

        def q_gmask():
            p.op("vector", lambda e: e.tensor_copy(
                sm[:, TC:TC + 1], sm[:, LO:LO + 1]))
            bcast_t()
            p.op("vector", lambda e: e.tensor_single_scalar(
                gcnt[0:ST, :], s16[0:ST, :], tT[0:ST, 0:1], ALU.is_gt))
            if ST < 32:
                p.op("vector", lambda e: e.memset(g16[0:32, :], 0.0))
            p.op("vector", lambda e: e.tensor_scalar(
                g16[0:ST, :], th16[0:ST, :], 0.5, 0.5, ALU.mult, ALU.add),
                waits=[("A", aq["cA_th"])])
            p.op("vector", lambda e: e.tensor_mul(
                g16[0:ST, :], g16[0:ST, :], gcnt[0:ST, :]))
            for bi in range(4):
                p.op("vector", lambda e, bi=bi: e.transpose(
                    g_sb[bi * 32:(bi + 1) * 32, 0:32],
                    g16[0:32, bi * 32:(bi + 1) * 32]))
        add(q_gmask)

        def q_scan1():
            cur, oth = scA, scB
            p.op("vector", lambda e, c=cur: e.tensor_copy(
                c[0:ST, :], gcnt[0:ST, :]))
            for sh in (1, 2, 4, 8, 16, 32, 64):
                p.op("vector", lambda e, sh=sh, c=cur, o=oth: e.tensor_copy(
                    o[0:ST, 0:sh], c[0:ST, 0:sh]))
                p.op("vector", lambda e, sh=sh, c=cur, o=oth: e.tensor_add(
                    o[0:ST, sh:128], c[0:ST, sh:128], c[0:ST, 0:128 - sh]))
                cur, oth = oth, cur
            aq["rank"], aq["u"] = cur, oth
        add(q_scan1)

        def q_scan2():
            rank = aq["rank"]
            p.op("vector", lambda e: e.memset(tpad[:, :], 0.0))
            p.op("vector", lambda e, c=rank: e.tensor_copy(
                tpad[0:ST, 0:1], c[0:ST, 127:128]))
            p.op("vector", lambda e: e.transpose(tT[:, :], tpad[:, :]))
            a_, b_ = tT, tpad
            for sh in (1, 2, 4, 8, 16):
                p.op("vector", lambda e, sh=sh, a=a_, b=b_: e.tensor_copy(
                    b[0:1, 0:sh], a[0:1, 0:sh]))
                p.op("vector", lambda e, sh=sh, a=a_, b=b_: e.tensor_add(
                    b[0:1, sh:32], a[0:1, sh:32], a[0:1, 0:32 - sh]))
                a_, b_ = b_, a_
            p.op("vector", lambda e, b=b_: e.memset(b[0:1, 0:1], 0.0))
            p.op("vector", lambda e, a=a_, b=b_: e.tensor_copy(
                b[0:1, 1:32], a[0:1, 0:31]))
            p.op("vector", lambda e, a=a_, b=b_: e.transpose(a[:, :], b[:, :]))
            p.op("vector", lambda e, c=rank, roff=a_: e.tensor_single_scalar(
                c[0:ST, :], c[0:ST, :], roff[0:ST, 0:1], ALU.add))
        add(q_scan2)

        def q_offs():
            rank, u = aq["rank"], aq["u"]
            p.op("vector", lambda e, c=rank, u=u: e.tensor_sub(
                u[0:ST, :], c[0:ST, :], gpos32[0:ST, :]))
            p.op("vector", lambda e, u=u: e.tensor_scalar(
                u[0:ST, :], u[0:ST, :], 1.0, float(S - 1), ALU.mult, ALU.add))
            p.op("vector", lambda e, c=rank, u=u: e.tensor_sub(
                s16[0:ST, :], c[0:ST, :], u[0:ST, :]))
            p.op("vector", lambda e: e.tensor_scalar(
                s16[0:ST, :], s16[0:ST, :], 1.0, -1.0, ALU.mult, ALU.add))
            p.op("vector", lambda e: e.tensor_mul(
                s16[0:ST, :], s16[0:ST, :], gcnt[0:ST, :]))
            p.op("vector", lambda e, u=u: e.tensor_add(
                u[0:ST, :], u[0:ST, :], s16[0:ST, :]))
            for b in range(4):
                p.op("vector", lambda e, b=b, u=u: e.transpose(
                    itmp[32 * b:32 * b + 32, 0:32], u[0:32, 32 * b:32 * b + 32]))
            p.op("vector", lambda e: e.tensor_copy(
                offs_u[:, 0:ST], itmp[:, 0:ST]))
            p.op("vector", lambda e: e.tensor_copy(
                comb_sb[:, 0:ST, 1:2],
                nmr_sb[:, 0:ST].rearrange("p (t c) -> p t c", c=1)))
            p.op("vector", lambda e: e.tensor_copy(
                comb_sb[:, 0:ST, 2:3],
                r_sb[:, 0:ST].rearrange("p (t c) -> p t c", c=1)))
            aq["c_comb"] = p.op("vector", lambda e: e.tensor_copy(
                comb_sb[:, 0:ST, 3:4],
                g_sb[:, 0:ST].rearrange("p (t c) -> p t c", c=1)))
        add(q_offs)

        def q_scatter():
            for ti in range(ST):
                w = ([("V", aq["c_comb"]), ("CR", prev_cr[0])] + w0()) \
                    if ti == 0 else []
                p.op("gpsimd", lambda e, ti=ti: e.indirect_dma_start(
                    out=combd, out_offset=IOA(offs_u[:, ti:ti + 1], 0),
                    in_=comb_sb[:, ti, :], in_offset=None),
                    waits=w, inc="CS", n=16)
            ccr = p.op("sync", lambda e: e.dma_start(
                out=comb2[:, :, :],
                in_=combd[0:SEL].rearrange("(k p) c -> p k c", p=128)),
                waits=[("CS", p.cnt["CS"])], inc="CR", n=16)
            prev_cr[0] = p.cnt["CR"]
            aov[rr] = {"c_idx": p.op("vector", lambda e: e.tensor_copy(
                idxu[:, 0:NMS], comb2[:, :, 0:1].rearrange("p k c -> p (k c)")),
                waits=[("CR", ccr)])}
        add(q_scatter)
        return qs


    def emit_row(r):
        q0s, n0 = [], [0]
        if r == 0:
            q0s = make_A_quanta(0)
            while n0[0] < 2 + ST + 1:
                q0s[n0[0]]()
                n0[0] += 1

        def drain0(k):
            while n0[0] < len(q0s) and k > 0:
                q0s[n0[0]]()
                n0[0] += 1
                k -= 1
        nxt_qs = make_A_quanta(r + 1) if r + 1 < R else []
        nd = [0]

        def drain(k):
            while nd[0] < len(nxt_qs) and k > 0:
                nxt_qs[nd[0]]()
                nd[0] += 1
                k -= 1
        row_es = contextlib.ExitStack()
        xnT_es = contextlib.ExitStack()
        xn_T = xnT_es.enter_context(
            nc.sbuf_tensor(f"xn_T_r{r}", [128, DT, S], BF16, side="right"))
        bar0 = p.snap()
        FB = lambda sn: [(k, sn[k]) for k in ("P", "A", "V") + DMAS]
        FBW = lambda sn: [(k, sn[k]) for k in ("P", "A", "V") + DMAS
                          if k not in ("OW0", "OW1", "BR0", "BR1", "BO")]

        wes = contextlib.ExitStack()
        FTH = FT // 2
        assert FTH % 2 == 0 or FTH == 2 or True
        W1_sb = wes.enter_context(
            nc.sbuf_tensor(f"W1_sb_r{r}", [128, FTH, DT, 128], BF16))
        W2_sb = wes.enter_context(
            nc.sbuf_tensor(f"W2_sb_r{r}", [128, FTH, D], BF16))
        GJ = 4
        NG = (FTH + GJ - 1) // GJ
        grp_done = {}
        for g in range(NG):
            for jj in range(GJ * g, min(GJ * (g + 1), FTH)):
                w = []
                if jj == GJ * g:
                    if g != 0:
                        w = [grp_done[g - 1]]
                    elif r == 0:
                        w = FBW(bar0)
                    else:
                        w = ([("P", rowtail[r - 1]["P"]),
                              ("V", rowtail[r - 1]["V"]),
                              ("A", bar0["A"])] +
                             [(k, bar0[k]) for k in DMAS
                              if k not in ("OW0", "OW1", "BR0", "BR1", "BO")])
                gs = f"DW{g % 2}"
                p.op("gpsimd", lambda e, j=jj: e.dma_start(
                    out=W1_sb[:, j], in_=w1_h[:, j]), waits=w, inc=gs, n=16)
                c = p.op("gpsimd", lambda e, j=jj: e.dma_start(
                    out=W2_sb[:, j], in_=w2_h[:, j]), inc=gs, n=16)
            grp_done[g] = (gs, c)
        w1c = {(0, j): grp_done[j // GJ] for j in range(FTH)}
        w2c = {(0, j): grp_done[j // GJ] for j in range(FTH)}
        wch = {}
        for g in range(NG):
            wch[f"DW{g % 2}"] = grp_done[g]
        for k in ("DW0", "DW1"):
            wch.setdefault(k, grp_done[NG - 1])


        with contextlib.ExitStack() as bes:
            bsb = lambda nm, shp, dt=F32: bes.enter_context(
                nc.sbuf_tensor(f"{nm}_r{r}", shp, dt))
            b2_sb = bsb("b2_sb", [128, D])
            cb2 = p.op("sync", lambda e: e.dma_start(out=b2_sb[:, :], in_=b2_h),
                       waits=FB(bar0), inc="DC", n=16)
            h1_sb = bsb("h1_sb", [128, FT, MT_B], BF16)
            ys = bsb("ys", [128, 8, MT_B])
            t2s = bsb("t2s", [128, 4, MT_B])
            zs = bsb("zs", [128, 4, MT_B])
            ths = bsb("ths", [128, 4, MT_B])
            xstB = bsb("xstB", [128, 2, D])
            ostB = bsb("ostB", [128, 2, D])
            xg = bsb("xg", [128, 2, D])
            xnbg = bsb("xnbg", [128, 2, D], BF16)
            xng_T = bsb("xng_T", [128, DT, SEL], BF16)
            th0_sb = bsb("th0_sb", [128, NMS, D], BF16)
            ps_h = [bes.enter_context(nc.psum_tensor(f"ps_h{k}_r{r}", [128, 512]))
                    for k in range(4)]
            ps_y = [bes.enter_context(nc.psum_tensor(f"ps_y{k}_r{r}", [128, 512]))
                    for k in range(ND5)]
            ps_tb = [bes.enter_context(
                nc.psum_tensor(f"ps_tb{k}_r{r}", [128, 512], BF16))
                     for k in range(2)]

            xnbr = bsb("xnbr", [128, 2, D], BF16)
            psb_last = {0: None, 1: None}
            b0_c, bw_done, xtr_c = {}, {}, {}

            def emit_xtile(ti):
                xsem = f"XB{ti % 2}"
                if ti >= 2:
                    w = [("A", b0_c[ti - 2]), bw_done[ti - 2]]
                elif r == 0:
                    w = FB(bar0)
                else:
                    w = ([("P", rowtail[r - 1]["P"]),
                          ("V", rowtail[r - 1]["V"]), ("A", bar0["A"])] +
                         [(k, bar0[k]) for k in DMAS
                          if k not in ("OW0", "OW1", "BR0", "BR1", "BO")])
                c = p.op("sync", lambda e, ti=ti, sl=ti % 2: e.dma_start(
                    out=xstB[:, sl, :], in_=x_hbm[r, ti * 128:(ti + 1) * 128, :]),
                    waits=w, inc=xsem, n=16)
                wa = [(xsem, c)]
                if ti >= 2:
                    wa.append(("P", xtr_c[(ti - 2, DT - 1)]))
                p.op("scalar", lambda e, ti=ti, sl=ti % 2: e.activation(
                    xnbr[:, sl, :], xstB[:, sl, :], AF.Identity,
                    bias=nmr_sb[:, ti:ti + 1], scale=r_sb[:, ti:ti + 1]),
                    waits=wa)
                b0_c[ti] = p.op("scalar", lambda e, ti=ti, sl=ti % 2: e.activation(
                    xstB[:, sl, :], xstB[:, sl, :], AF.Identity,
                    bias=nmr_sb[:, ti:ti + 1], scale=r1p_sb[:, ti:ti + 1]))
                c2 = p.op("sync", lambda e, ti=ti, sl=ti % 2: e.dma_start(
                    out=out_hbm[r, ti * 128:(ti + 1) * 128, :],
                    in_=xstB[:, sl, :]),
                    waits=[("A", b0_c[ti])], inc=f"BW{ti % 2}", n=16)
                bw_done[ti] = (f"BW{ti % 2}", c2)
                for dt in range(DT):
                    w2 = [("A", b0_c[ti])] if dt == 0 else []
                    if ti == 0 and dt == 0:
                        w2 += FB(bar0)
                    if dt % 4 == 0 and psb_last[dt // 4] is not None:
                        w2.append(psb_last[dt // 4])
                    xtr_c[(ti, dt)] = p.op("tensor",
                        lambda e, ti=ti, dt=dt: e.transpose(
                            ps_tb[dt // 4][:, (dt % 4) * 128:(dt % 4 + 1) * 128],
                            xnbr[:, ti % 2, dt * 128:(dt + 1) * 128],
                            id_sb[:, :]), waits=w2, inc="P")
                for q in range(2):
                    c3 = p.op("vector", lambda e, ti=ti, q=q: e.tensor_copy(
                        xn_T[:, 4 * q:4 * q + 4, ti * 128:(ti + 1) * 128],
                        ps_tb[q][:, 0:512].rearrange("p (a t) -> p a t", t=128)),
                        waits=[("P", xtr_c[(ti, 4 * q + 3)])])
                    psb_last[q] = ("V", c3)


            xnbg_c, tpm_c, tpe_c = {}, {}, {}

            def prep(k):
                w = [("V", aov[r]["c_idx"])]
                if k >= 2:
                    w.append(("A", xnbg_c[k - 2]))
                cg = p.op("gpsimd", lambda e, k=k: e.indirect_dma_start(
                    out=xg[:, k % 2, :], out_offset=None,
                    in_=x_hbm.rearrange("r s d -> (r s) d"),
                    in_offset=IOA(idxu[:, k:k + 1], 0),
                    element_offset=r * S * D),
                    waits=w, inc=f"XG{k % 2}", n=16)
                wa = [(f"XG{k % 2}", cg)]
                if k >= 2:
                    wa.append(("P", tpm_c[(k - 2, DT - 1)]))
                xnbg_c[k] = p.op("scalar", lambda e, k=k: e.activation(
                    xnbg[:, k % 2, :], xg[:, k % 2, :], AF.Identity,
                    bias=comb2[:, k, 1:2], scale=comb2[:, k, 2:3]), waits=wa)
                for dt in range(DT):
                    w2 = [("A", xnbg_c[k])] if dt == 0 else []
                    if k == 0 and dt == 0:
                        w2 += FB(bar0)
                    if dt % 4 == 0 and psb_last[dt // 4] is not None:
                        w2.append(psb_last[dt // 4])
                    tpm_c[(k, dt)] = p.op("tensor",
                        lambda e, k=k, dt=dt: e.transpose(
                            ps_tb[dt // 4][:, (dt % 4) * 128:(dt % 4 + 1) * 128],
                            xnbg[:, k % 2, dt * 128:(dt + 1) * 128],
                            id_sb[:, :]), waits=w2, inc="P")
                for q in range(2):
                    tpe_c[(k, q)] = p.op("scalar", lambda e, k=k, q=q: e.activation(
                        xng_T[:, 4 * q:4 * q + 4, k * 128:(k + 1) * 128],
                        ps_tb[q][:, 0:512].rearrange("p (a t) -> p a t", t=128),
                        AF.Identity),
                        waits=[("P", tpm_c[(k, 4 * q + 3)])])
                    psb_last[q] = ("A", tpe_c[(k, q)])

            h1mm_c, h1bf_c, yev_c = {}, {}, {}
            sq_c2, t2_c2, z_c2, th_c2, w_c2 = {}, {}, {}, {}, {}
            o2mm_c, t1_c, gm_c, sa_done, fin_c = {}, {}, {}, {}, {}
            GSEQH = NMS * FTH
            GSEQ2 = 2 * GSEQH

            def gdec(g):
                ph, gg = divmod(g, GSEQH)
                m, jl = divmod(gg, FTH)
                return ph, m, jl, ph * FTH + jl, ph * NMS + m

            def st_h1mm(g):
                ph, m, jl, jg, M2 = gdec(g)
                sl = g % 4
                for k in range(DT):
                    w = []
                    if k == 0:
                        w = [w1c[(ph, jl)]]
                        if jl == 0:
                            w.append(("A", tpe_c[(m, 1)]))
                        if g >= 4:
                            w.append(("A", yev_c[g - 4]))
                    h1mm_c[g] = p.op("tensor",
                        lambda e, jl=jl, k=k, m=m, sl=sl: e.matmul(
                            ps_h[sl][:, 0:MT_B], W1_sb[:, jl, k],
                            xng_T[:, k, m * MT_B:(m + 1) * MT_B],
                            start=(k == 0), stop=(k == DT - 1)), waits=w, inc="P")
                if ph == 0 and m == NMS - 1:
                    gs = f"DW{jl % 2}"
                    c = p.op("gpsimd", lambda e, jl=jl: e.dma_start(
                        out=W1_sb[:, jl], in_=w1_h[:, FTH + jl]),
                        waits=[wch[gs], ("P", h1mm_c[g])], inc=gs, n=16)
                    wch[gs] = (gs, c)
                    w1c[(1, jl)] = (gs, c)

            def st_yev(g):
                ph, m, jl, jg, M2 = gdec(g)
                w = [("P", h1mm_c[g])]
                if g >= 8:
                    w += [("G", sq_c2[g - 8]), ("V", h1bf_c[g - 8])]
                yev_c[g] = p.op("scalar", lambda e, jg=jg, sl=g % 4, ysl=g % 8:
                    e.activation(ys[:, ysl, 0:MT_B], ps_h[sl][:, 0:MT_B],
                                 AF.Identity, bias=b1_sb[:, jg:jg + 1]), waits=w)

            def st_sq(g):
                if g % 2 == 0:
                    return
                ysl, tsl = (g - 1) % 8, (g - 1) % 4
                yv = ys[:, ysl:ysl + 2, :].rearrange("p a t -> p (a t)")
                tv = t2s[:, tsl:tsl + 2, :].rearrange("p a t -> p (a t)")
                w = [("A", yev_c[g]), ("G", p.cnt["G"])]
                if g >= 4:
                    w.append(("V", z_c2[g - 4]))
                c = p.op("gpsimd", lambda e, yv=yv, tv=tv: e.tensor_mul(
                    tv, yv, yv), waits=w, inc="G")
                sq_c2[g] = sq_c2[g - 1] = c
                c = p.op("vector", lambda e, tv=tv: e.tensor_scalar(
                    tv, tv, GA * GB, GA, ALU.mult, ALU.add),
                    waits=[("G", c)])
                t2_c2[g] = t2_c2[g - 1] = ("V", c)

            def st_z(g):
                if g % 2 == 0:
                    return
                ysl, tsl = (g - 1) % 8, (g - 1) % 4
                yv = ys[:, ysl:ysl + 2, :].rearrange("p a t -> p (a t)")
                tv = t2s[:, tsl:tsl + 2, :].rearrange("p a t -> p (a t)")
                zv = zs[:, tsl:tsl + 2, :].rearrange("p a t -> p (a t)")
                w = [t2_c2[g]]
                if g >= 4:
                    w.append(("A", th_c2[g - 4]))
                c = p.op("vector", lambda e, zv=zv, yv=yv, tv=tv: e.tensor_mul(
                    zv, yv, tv), waits=w)
                z_c2[g] = z_c2[g - 1] = c

            def st_tanh(g):
                if g % 2 == 0:
                    return
                tsl = (g - 1) % 4
                zv = zs[:, tsl:tsl + 2, :].rearrange("p a t -> p (a t)")
                hv = ths[:, tsl:tsl + 2, :].rearrange("p a t -> p (a t)")
                w = [("V", z_c2[g])]
                if g >= 4:
                    w += [w_c2[g - 4], ("V", h1bf_c[g - 4])]
                c = p.op("scalar", lambda e, hv=hv, zv=zv: e.activation(
                    hv, zv, AF.Tanh), waits=w)
                th_c2[g] = th_c2[g - 1] = c

            def st_w(g):
                if g % 2 == 0:
                    return
                tsl = (g - 1) % 4
                hv = ths[:, tsl:tsl + 2, :].rearrange("p a t -> p (a t)")
                c = p.op("gpsimd", lambda e, hv=hv: e.tensor_scalar(
                    hv, hv, 0.5, 0.5, ALU.mult, ALU.add),
                    waits=[("A", th_c2[g]), ("G", p.cnt["G"])], inc="G")
                w_c2[g] = w_c2[g - 1] = ("G", c)

            def st_h1bf(g):
                if g % 2 == 0:
                    return
                ph, m, jl, jg, M2 = gdec(g)
                ysl, tsl, jl0 = (g - 1) % 8, (g - 1) % 4, jl - 1
                yv = ys[:, ysl:ysl + 2, :].rearrange("p a t -> p (a t)")
                hv = ths[:, tsl:tsl + 2, :].rearrange("p a t -> p (a t)")
                h1v = h1_sb[:, jl0:jl0 + 2, :].rearrange("p a t -> p (a t)")
                w = [w_c2[g]]
                if M2 >= 1:
                    w.append(("P", o2mm_c[(M2 - 1, jl, ND5 - 1)]))
                c = p.op("vector", lambda e, h1v=h1v, yv=yv, hv=hv: e.tensor_mul(
                    h1v, yv, hv), waits=w)
                h1bf_c[g] = h1bf_c[g - 1] = c

            def st_out2(g):
                ph, m, jl, jg, M2 = gdec(g)
                for nch in range(ND5):
                    w = []
                    if nch == 0:
                        w = [w2c[(ph, jl)], ("V", h1bf_c[g])]
                        if jl == 0 and M2 >= 1:
                            w.append(("V", fin_c[M2 - 1]))
                    o2mm_c[(M2, jl, nch)] = p.op("tensor",
                        lambda e, jl=jl, nch=nch: e.matmul(
                            ps_y[nch][:, 0:512],
                            h1_sb[:, jl, 0:MT_B],
                            W2_sb[:, jl, nch * 512:(nch + 1) * 512],
                            start=(jl == 0), stop=(jl == FTH - 1)), waits=w, inc="P")
                if ph == 0 and m == NMS - 1:
                    gs = f"DW{jl % 2}"
                    c = p.op("gpsimd", lambda e, jl=jl: e.dma_start(
                        out=W2_sb[:, jl], in_=w2_h[:, FTH + jl]),
                        waits=[wch[gs], ("P", o2mm_c[(M2, jl, ND5 - 1)])],
                        inc=gs, n=16)
                    wch[gs] = (gs, c)
                    w2c[(1, jl)] = (gs, c)
                if jl == FTH - 1:
                    if ph == 0:
                        emit_th0(m)
                    else:
                        emit_upd(m)

            def emit_th0(m):
                last = 0
                for nch in range(ND5):
                    w = [("P", o2mm_c[(m, FTH - 1, nch)])]
                    last = p.op("vector", lambda e, m=m, nch=nch: e.tensor_copy(
                        th0_sb[:, m, nch * 512:(nch + 1) * 512],
                        ps_y[nch][:, 0:512]), waits=w)
                fin_c[m] = last

            def emit_upd(m):
                sl = m % 2
                M2 = NMS + m
                for nch in range(ND5):
                    w = [("P", o2mm_c[(M2, FTH - 1, nch)]), ("DC", cb2)]
                    if m >= 2:
                        w.append(sa_done[m - 2])
                    p.op("vector", lambda e, nch=nch, sl=sl: e.tensor_add(
                        ostB[:, sl, nch * 512:(nch + 1) * 512],
                        ps_y[nch][:, 0:512],
                        b2_sb[:, nch * 512:(nch + 1) * 512]), waits=w)
                    t1_c[(m, nch)] = p.op("vector",
                        lambda e, m=m, nch=nch, sl=sl: e.tensor_add(
                            ostB[:, sl, nch * 512:(nch + 1) * 512],
                            ostB[:, sl, nch * 512:(nch + 1) * 512],
                            th0_sb[:, m, nch * 512:(nch + 1) * 512]))
                gm_c[m] = p.op("vector", lambda e, m=m, sl=sl: e.tensor_single_scalar(
                    ostB[:, sl, :], ostB[:, sl, :], comb2[:, m, 3:4], ALU.mult))
                fin_c[M2] = gm_c[m]
                wsa = [("V", gm_c[m])] + [bw_done[t_] for t_ in
                                          (NM_B - 2, NM_B - 1) if t_ in bw_done]
                assert len(bw_done) == NM_B, "scatter before base0 fully emitted"
                csa = p.op("gpsimd", lambda e, m=m, sl=sl: e.indirect_dma_start(
                    out=out_hbm.rearrange("r s d -> (r s) d"),
                    out_offset=IOA(idxu[:, m:m + 1], 0),
                    in_=ostB[:, sl, :], in_offset=None, compute_op=ALU.add,
                    element_offset=r * S * D),
                    waits=wsa, inc=f"SA{m % 2}", n=16)
                sa_done[m] = (f"SA{m % 2}", csa)

            if r == 0:
                emit_xtile(0)
                emit_xtile(1)
                for _ti in range(2, NM_B):
                    drain0(4)
                    emit_xtile(_ti)
                drain0(len(q0s))
            prep(0)
            if NMS > 1:
                prep(1)
            if r > 0:
                emit_xtile(0)
                emit_xtile(1)
            xt_next = [NM_B if r == 0 else 2]
            XT_EVERY = max(1, (GSEQH - FTH) // max(1, NM_B - 2))
            LAG_O2 = min(12, max(7, FTH + 4))
            stages = ((0, st_h1mm), (1, st_yev), (2, st_sq), (3, st_z),
                      (4, st_tanh), (5, st_w), (6, st_h1bf), (LAG_O2, st_out2))
            for it in range(GSEQ2 + LAG_O2):
                if it % FTH == 0 and it // FTH + 2 < NMS:
                    prep(it // FTH + 2)
                if it % XT_EVERY == XT_EVERY - 1 and xt_next[0] < NM_B:
                    emit_xtile(xt_next[0])
                    xt_next[0] += 1
                for lag, fn in stages:
                    g = it - lag
                    if 0 <= g < GSEQ2:
                        fn(g)
            while xt_next[0] < NM_B:
                emit_xtile(xt_next[0])
                xt_next[0] += 1
            cBW = [("BW0", p.cnt["BW0"]), ("BW1", p.cnt["BW1"]),
                   ("SA0", p.cnt["SA0"]), ("SA1", p.cnt["SA1"])]
        wes.close()
        barB = p.snap()

        with contextlib.ExitStack() as ces:
            csb = lambda nm, shp, dt=F32: ces.enter_context(
                nc.sbuf_tensor(f"{nm}_r{r}", shp, dt))
            q_T = csb("q_T", [128, DT, S], BF16)
            k_T = csb("k_T", [128, DT, S], BF16)
            v_aug = csb("v_aug", [128, ST, H * 66], BF16)
            o_sb = csb("o_sb", [64, H, QCH], BF16)

            with contextlib.ExitStack() as c2:
                c2sb = lambda nm, shp, dt=F32: c2.enter_context(
                    nc.sbuf_tensor(f"{nm}_r{r}", shp, dt))
                bv_sb = c2sb("bv_sb", [128, D])
                wbuf = [c2sb("wbuf0", [128, DT, D], BF16),
                        c2sb("wbuf1", [128, DT, D], BF16)]
                ps_c4 = c2.enter_context(nc.psum_tensor(f"ps_c4_r{r}", [128, 2048]))
                wB = FB(barB)
                c0 = p.op("sync", lambda e: e.dma_start(out=bv_sb[:, :], in_=bv_h),
                     waits=wB, inc="XA0", n=16)
                cDS_bias = [("XA0", c0)]
                c = p.op("gpsimd", lambda e: e.dma_start(
                    out=wbuf[0][:, :, :], in_=wqT_h.rearrange("(k p) d -> p k d", p=128)),
                    waits=wB, inc="WQ", n=16)
                cw_q = ("WQ", c)
                vv2 = v_aug[:, :, :].rearrange("p t (g wc) -> p t g wc", wc=132)
                p.op("vector", lambda e: e.memset(vv2[:, :, :, 64:65], 1.0),
                     waits=FB(barB))
                p.op("vector", lambda e: e.memset(vv2[:, :, :, 65:66], 0.0))
                p.op("vector", lambda e: e.memset(vv2[:, :, :, 130:131], 0.0))
                cV_ones = p.op("vector", lambda e: e.memset(
                    vv2[:, :, :, 131:132], 1.0))

                NS5 = S // 512
                mmq_c, evq_c = {}, {}

                def emit_proj(dst, wslot, bias, wready, tag, prev_tag):
                    for d in range(DT):
                        for n in range(NS5):
                            seq = d * NS5 + n
                            sl = seq % 4
                            for k in range(DT):
                                w = []
                                if k == 0:
                                    if seq == 0:
                                        w.append(wready)
                                    if seq >= 4:
                                        w.append(evq_c[(tag, seq - 4 +
                                                        (1 if NS5 >= 2 else 0))])
                                    elif prev_tag is not None:
                                        w.append(evq_c[(prev_tag, DT * NS5 - 1)])
                                        if NS5 >= 2:
                                            w.append(evq_c[(prev_tag,
                                                            DT * NS5 - 3)])
                                    else:
                                        w += [("V", barB["V"]), ("A", barB["A"])]
                                mmq_c[(tag, seq)] = p.op("tensor",
                                    lambda e, d=d, n=n, k=k, sl=sl, ws_=wslot: e.matmul(
                                        ps_c4[:, sl * 512:(sl + 1) * 512],
                                        wbuf[ws_][:, k, d * 128:(d + 1) * 128],
                                        xn_T[:, k, n * 512:(n + 1) * 512],
                                        start=(k == 0), stop=(k == DT - 1)),
                                    waits=w, inc="P")
                            if NS5 < 2:
                                evq_c[(tag, seq)] = ("A", p.op("scalar",
                                    lambda e, d=d, n=n, sl=sl, dst=dst, bias=bias:
                                    e.activation(
                                        dst[:, d, n * 512:(n + 1) * 512],
                                        ps_c4[:, sl * 512:(sl + 1) * 512],
                                        AF.Identity, bias=bias[:, d:d + 1]),
                                    waits=[("P", mmq_c[(tag, seq)])]))
                                continue
                            if seq % 2 == 1:
                                sl0, n0 = sl - 1, n - 1
                                dv = dst[:, d, n0 * 512:(n0 + 2) * 512]
                                pv = ps_c4[:, sl0 * 512:(sl0 + 2) * 512]
                                if (seq // 2) % 2 == 0:
                                    c = ("A", p.op("scalar",
                                        lambda e, dv=dv, pv=pv, d=d, bias=bias:
                                        e.activation(dv, pv, AF.Identity,
                                                     bias=bias[:, d:d + 1]),
                                        waits=[("P", mmq_c[(tag, seq)])]))
                                else:
                                    c = ("V", p.op("vector",
                                        lambda e, dv=dv, pv=pv, d=d, bias=bias:
                                        e.tensor_single_scalar(
                                            dv, pv, bias[:, d:d + 1], ALU.add),
                                        waits=[("P", mmq_c[(tag, seq)])]))
                                evq_c[(tag, seq)] = evq_c[(tag, seq - 1)] = c

                c = p.op("gpsimd", lambda e: e.dma_start(
                    out=wbuf[1][:, :, :],
                    in_=wkT_h.rearrange("(k p) d -> p k d", p=128)),
                    waits=wB, inc="WK", n=16)
                cw_k = ("WK", c)
                emit_proj(q_T, 0, bq_sb, cw_q, "q", None)
                emit_proj(k_T, 1, bk_sb, cw_k, "k", "q")
                cw_vk = {}
                for kv in range(DT):
                    c = p.op("gpsimd", lambda e, kv=kv: e.dma_start(
                        out=wbuf[0][:, kv, :],
                        in_=wvT_h.rearrange("(k p) d -> p k d", p=128)[:, kv]),
                        waits=([("P", mmq_c[("q", DT * NS5 - 1)])]
                               if kv == 0 else [cw_vk[kv - 1]]),
                        inc="WV", n=16)
                    cw_vk[kv] = ("WV", c)
                cw_v = cw_vk[DT - 1]

                mmv_c, evv_c = {}, {}
                for mt in range(ST):
                    for n in range(ND5):
                        seq = mt * ND5 + n
                        sl = seq % 4
                        for k in range(DT):
                            w = []
                            if seq == 0:
                                w.append(cw_vk[k])
                            if k == 0:
                                if seq >= 4:
                                    w.append(("V", evv_c[seq - 4 + 1]))
                                elif seq < 4:
                                    w.append(evq_c[("k", DT * NS5 - 1)])
                                    if NS5 >= 2:
                                        w.append(evq_c[("k", DT * NS5 - 3)])
                            mmv_c[seq] = p.op("tensor",
                                lambda e, mt=mt, n=n, k=k, sl=sl: e.matmul(
                                    ps_c4[:, sl * 512:(sl + 1) * 512],
                                    xn_T[:, k, mt * 128:(mt + 1) * 128],
                                    wbuf[0][:, k, n * 512:(n + 1) * 512],
                                    start=(k == 0), stop=(k == DT - 1)), waits=w, inc="P")
                        if seq % 2 == 1:
                            sl0, n0 = sl - 1, n - 1
                            dstv = v_aug[:, mt].rearrange(
                                "p (h c) -> p h c", h=H)[
                                :, n0 * 8:(n0 + 2) * 8, 0:64]
                            srcv = ps_c4[:, sl0 * 512:(sl0 + 2) * 512]
                            evv_c[seq] = evv_c[seq - 1] = p.op("vector",
                                lambda e, dstv=dstv, srcv=srcv, n0=n0: e.tensor_add(
                                    dstv, srcv.rearrange("p (h c) -> p h c", h=16),
                                    bv_sb[:, n0 * 512:(n0 + 2) * 512].rearrange(
                                        "p (h c) -> p h c", h=16)),
                                waits=[("P", mmv_c[seq])] + cDS_bias +
                                      [("V", cV_ones)])
                cA_C = p.cnt["A"]
                cV_C = p.cnt["V"]
            barC = p.snap()
            xnT_es.close()

            with contextlib.ExitStack() as des:
                dsb = lambda nm, shp, dt=F32: des.enter_context(
                    nc.sbuf_tensor(f"{nm}_r{r}", shp, dt, side="right"))
                attn_sb = dsb("attn_sb", [128, 6, 2, QCH], BF16)
                rs32 = dsb("rs32", [128, QCH])
                rsT = dsb("rsT", [128, QCH])
                rhi_sb = dsb("rhi_sb", [128, QCH], BF16)
                rlo_sb = dsb("rlo_sb", [128, QCH], BF16)
                o_un_t = dsb("o_un", [64, 2, QCH], BF16)
                woT_sb = dsb("woT_sb", [128, H // 2, D], BF16)
                o2_sb = dsb("o2_sb", [128, H // 2, QCH], BF16)
                bo_sb = dsb("bo_sb", [128, D])
                ostE = dsb("ostE", [128, 2, 512])
                bstE = dsb("bstE", [128, 2, 2, 512])
                ps_sr = [des.enter_context(
                    nc.psum_tensor(f"ps_s{k}_r{r}", [128, 2 * QCH]))
                    for k in range(2)]
                ps_o = [des.enter_context(nc.psum_tensor(f"ps_o{k}_r{r}", [128, QCH]))
                        for k in range(4)]

                cbo = p.op("sync", lambda e: e.dma_start(out=bo_sb[:, :], in_=bo_h),
                           waits=FB(barC), inc="BO", n=16)
                p.op("vector", lambda e: e.memset(rs32[64:96, 0:QCH], 1.0),
                     waits=[("P", barC["P"])])
                c = p.op("gpsimd", lambda e: e.dma_start(
                    out=woT_sb[:, :, :], in_=woTr_h), waits=FB(barC), inc="WO", n=16)
                cw_o = ("WO", c)
                ev_done = {}
                ow_done = {}
                prev_ev_last = [0, 0]
                prev_av_last = [0]
                prev_E_last = [0]

                o_un = o_un_t
                prev_outer_last = [0]
                for qh in range(NQ):
                    sc_c, ex_c, av_c, mul_c = {}, {}, {}, {}
                    hilo_c, outer_c, oun_c = {}, {}, {}

                    def emit_scpair(t, pi):
                        h0, h1 = 2 * pi, 2 * pi + 1
                        psd = ps_sr[t % 2]
                        w = []
                        if t >= 2:
                            w = [("A", ex_c[(pi, t - 2)])]
                        elif pi > 0:
                            w = [("A", ex_c[(pi - 1, ST - 2 + t)])]
                        elif qh > 0:
                            w = [("V", prev_ev_last[t % 2])]
                        else:
                            w = FB(barC)
                        for par, h in ((0, h0), (1, h1)):
                            for c in range(NC5):
                                sc_c[(h, t, c)] = p.op("tensor",
                                    lambda e, h=h, t=t, c=c, hd=h // 2, psd=psd,
                                    koff=64 * (h % 2), qh=qh: e.matmul(
                                        psd[:, (h % 2) * QCH + c * 512:
                                            (h % 2) * QCH + (c + 1) * 512],
                                        k_T[koff:koff + 64, hd, t * 128:(t + 1) * 128],
                                        q_T[koff:koff + 64, hd,
                                            qh * QCH + c * 512:qh * QCH + (c + 1) * 512],
                                        start=True, stop=True), waits=w, inc="P")
                                w = []
                        wex = [("P", sc_c[(h1, t, NC5 - 1)])]
                        if t >= 6:
                            wex.append(("P", av_c[(h1, t - 6, NC5 - 1)]))
                        elif pi > 0:
                            tp = t + 6 * ((ST - 1 - t) // 6)
                            wex.append(("P", av_c[(h1 - 2, tp, NC5 - 1)]))
                        elif qh > 0:
                            wex.append(("P", prev_av_last[0]))
                        else:
                            wex += [("A", barC["A"]), ("V", barC["V"])]
                        ex_c[(pi, t)] = p.op("scalar",
                            lambda e, t=t, psd=psd, ar=t % 6: e.activation(
                                attn_sb[:, ar, 0:2, 0:QCH].rearrange(
                                    "p a q -> p (a q)"),
                                psd[:, 0:2 * QCH], AF.Exp, scale=0.125), waits=wex)

                    def emit_av4(t, pi):
                        g = 2 * (pi % 2)
                        for par, h in ((0, 2 * pi), (1, 2 * pi + 1)):
                            w = [("A", ex_c[(pi, t)])]
                            if t == 0:
                                if pi > 1:
                                    w += [("V", mul_c[(2 * pi - 4, qh)]),
                                          ("V", mul_c[(2 * pi - 3, qh)])]
                                elif qh > 0:
                                    w.append(("V", mul_prev[pi % 2]))
                                else:
                                    w.append(("V", cV_C))
                            for c in range(NC5):
                                av_c[(h, t, c)] = p.op("tensor",
                                    lambda e, h=h, t=t, c=c, g=g, ar=t % 6: e.matmul(
                                        ps_o[g + h % 2][0:66, c * 512:(c + 1) * 512],
                                        v_aug[:, t, h * 66:(h + 1) * 66],
                                        attn_sb[:, ar, h % 2, c * 512:(c + 1) * 512],
                                        start=(t == 0), stop=(t == ST - 1)),
                                    waits=w, inc="P")
                                w = []

                    def emit_outer(hh):
                        tgt = 2 * ((hh // 2) % 2) + (1 - hh % 2)
                        last = 0
                        for c in range(NC5):
                            w = [("V", oun_c[hh // 2])] if c == 0 else []
                            p.op("tensor", lambda e, c=c, hh=hh, tgt=tgt: e.matmul(
                                ps_o[tgt][0:64, c * 512:(c + 1) * 512],
                                pick_sb[64:66, (hh % 2) * 64:(hh % 2) * 64 + 64],
                                rhi_sb[64:66, c * 512:(c + 1) * 512],
                                start=True, stop=False), waits=w, inc="P")
                            last = p.op("tensor", lambda e, c=c, hh=hh, tgt=tgt:
                                e.matmul(
                                    ps_o[tgt][0:64, c * 512:(c + 1) * 512],
                                    pick_sb[64:66, (hh % 2) * 64:(hh % 2) * 64 + 64],
                                    rlo_sb[64:66, c * 512:(c + 1) * 512],
                                    start=False, stop=True), inc="P")
                        outer_c[hh] = last

                    def emit_norm_tail(hh):
                        tgt = 2 * ((hh // 2) % 2) + (1 - hh % 2)
                        mul_c[(hh, qh)] = p.op("vector", lambda e, hh=hh, tgt=tgt:
                            e.tensor_mul(
                                o_sb[0:64, hh, 0:QCH], o_un[0:64, hh % 2, 0:QCH],
                                ps_o[tgt][0:64, 0:QCH]),
                            waits=[("P", outer_c[hh])])

                    def emit_dance(pi):
                        h0, h1 = 2 * pi, 2 * pi + 1
                        g = 2 * (pi % 2)
                        wo_ = [("P", av_c[(h1, ST - 1, NC5 - 1)])]
                        if pi == 0 and qh > 0:
                            wo_.append(("P", prev_E_last[0]))
                        p.op("vector", lambda e, g=g: e.tensor_copy(
                            rs32[64:66, 0:QCH], ps_o[g + 1][64:66, 0:QCH]), waits=wo_)
                        p.op("vector", lambda e, g=g: e.tensor_copy(
                            rs32[64:65, 0:QCH], ps_o[g][64:65, 0:QCH]),
                            waits=[("P", av_c[(h0, ST - 1, NC5 - 1)])])
                        p.op("vector", lambda e: e.transpose(
                            rsT[64:96, 0:QCH], rs32[64:96, 0:QCH]))
                        p.op("vector", lambda e: e.reciprocal(
                            rsT[64:96, 0:QCH].rearrange(
                                "p (c j) -> p c j", j=32)[:, :, 0:2],
                            rsT[64:96, 0:QCH].rearrange(
                                "p (c j) -> p c j", j=32)[:, :, 0:2]))
                        p.op("vector", lambda e: e.transpose(
                            rs32[64:96, 0:QCH], rsT[64:96, 0:QCH]))
                        p.op("vector", lambda e: e.tensor_copy(
                            rhi_sb[64:66, 0:QCH], rs32[64:66, 0:QCH]),
                            waits=[("P", prev_outer_last[0])])
                        hilo_c[pi] = p.op("vector", lambda e: e.tensor_sub(
                            rlo_sb[64:66, 0:QCH], rs32[64:66, 0:QCH],
                            rhi_sb[64:66, 0:QCH]))
                        p.op("vector", lambda e, g=g: e.tensor_copy(
                            o_un[0:64, 0, 0:QCH], ps_o[g][0:64, 0:QCH]))
                        oun_c[pi] = p.op("vector", lambda e, g=g: e.tensor_copy(
                            o_un[0:64, 1, 0:QCH], ps_o[g + 1][0:64, 0:QCH]))

                    OUT_T = min(7, ST - 1)
                    DAN_T = min(1, ST - 1)
                    due, tail_avs = {}, []
                    for ta in range(ST):
                        e_s = ta + 3
                        if e_s <= ST - 1:
                            due.setdefault(e_s, []).append(ta)
                        else:
                            tail_avs.append(ta)
                    for pi in range(H // 2):
                        for t in range(ST):
                            for ta in due.get(t, []):
                                emit_av4(ta, pi)
                            emit_scpair(t, pi)
                            if t == 0 and pi > 0 and ST >= 16:
                                for ta in tail_avs:
                                    emit_av4(ta, pi - 1)
                            if t == DAN_T and pi > 0:
                                emit_dance(pi - 1)
                            if t == DAN_T:
                                drain(2)
                            if t == OUT_T and pi > 0:
                                emit_outer(2 * pi - 1)
                                emit_norm_tail(2 * pi - 1)
                                emit_outer(2 * pi - 2)
                                emit_norm_tail(2 * pi - 2)
                                prev_outer_last[0] = outer_c[2 * pi - 2]
                        if ST < 16:
                            for ta in tail_avs:
                                emit_av4(ta, pi)
                    if ST >= 16:
                        for ta in tail_avs:
                            emit_av4(ta, H // 2 - 1)
                    emit_dance(H // 2 - 1)
                    NMq = QCH // 128
                    EROT = 2 if QCH >= 1024 else 1

                    def psE(mt, nch):
                        s4 = (mt * ND5 + nch) % 4
                        return ps_sr[s4 // 2][:, (s4 % 2) * 512:(s4 % 2 + 1) * 512]

                    evE = {}

                    def e_read(mt):
                        ti = qh * NMq + mt
                        wbr = list(cBW)
                        if qh == 0 and mt < 2:
                            wbr += FB(barC)
                        if ti >= 2:
                            wbr.append(("V", ev_done.get(ti - 2, 0)))
                        c = p.op("sync", lambda e, ti=ti, sl=mt % 2:
                            e.dma_start(
                                out=bstE[:, sl, :, :].rearrange(
                                    "p a c -> p (a c)"),
                                in_=out_hbm[r, ti * 128:(ti + 1) * 128, :]),
                            waits=wbr, inc=f"BR{mt % 2}", n=16)
                        return [(f"BR{mt % 2}", c)]

                    def e_mms(mt, hps, extra):
                        last = 0
                        ti = qh * NMq + mt
                        for nch in range(ND5):
                            for hp in hps:
                                w = []
                                if hp == 0:
                                    w = list(extra) + [cw_o,
                                         ("A", ex_c[(H // 2 - 1, ST - 1)])]
                                    if nch == 0 and ti >= 2:
                                        w.append(("V", ev_done.get(ti - 2, 0)))
                                elif hp == H // 2 - 1:
                                    w = list(cw_oc[0])
                                last = p.op("tensor",
                                    lambda e, mt=mt, nch=nch, hp=hp: e.matmul(
                                        ps_sr[mt % 2][:, nch * 512:(nch + 1) * 512],
                                        o2_sb[:, hp, mt * 128:(mt + 1) * 128],
                                        woT_sb[:, hp, nch * 512:(nch + 1) * 512],
                                        start=(hp == 0), stop=(hp == H // 2 - 1)),
                                    waits=w, inc="P")
                        return last

                    def e_tail(mt, dr_cs, lastmm):
                        ti = qh * NMq + mt
                        wev = [("P", lastmm), ("BO", cbo)]
                        if ti >= 1 and (ti - 1) in ow_done:
                            wev.append(ow_done[ti - 1])
                        p.op("vector", lambda e, mt=mt: e.tensor_add(
                            ostE[:, :, :].rearrange("p a c -> p (a c)"),
                            ps_sr[mt % 2][:, 0:2 * 512],
                            bo_sb[:, 0:1024]), waits=wev)
                        evE[mt] = p.op("vector", lambda e, sl=mt % 2:
                            e.tensor_add(
                                ostE[:, :, :].rearrange("p a c -> p (a c)"),
                                ostE[:, :, :].rearrange("p a c -> p (a c)"),
                                bstE[:, sl, :, :].rearrange("p a c -> p (a c)")),
                            waits=dr_cs)
                        ev_done[ti] = evE[mt]
                        c = p.op("sync", lambda e, ti=ti: e.dma_start(
                            out=out_hbm[r, ti * 128:(ti + 1) * 128, :],
                            in_=ostE[:, :, :].rearrange("p a c -> p (a c)")),
                            waits=[("V", evE[mt])], inc="OW0", n=16)
                        ow_done[ti] = ("OW0", c)
                        prev_E_last[0] = lastmm

                    osp = o_sb[:, :, 0:QCH].rearrange(
                        "p (h two) q -> p h two q", two=2)
                    HP = H // 2
                    woc = [("V", mul_c[(H - 4, qh)]),
                           ("V", mul_c[(H - 3, qh)]),
                           ("P", prev_E_last[0])]
                    oa0 = p.op("sync", lambda e, osp=osp, HP=HP: e.dma_start(
                        out=o2_sb[0:64, 0:HP - 1, :],
                        in_=osp[:, 0:HP - 1, 0, :]),
                        waits=woc, inc="OC0", n=16)
                    oa1 = p.op("sync", lambda e, osp=osp, HP=HP: e.dma_start(
                        out=o2_sb[64:128, 0:HP - 1, :],
                        in_=osp[:, 0:HP - 1, 1, :]),
                        waits=woc, inc="OC1", n=16)
                    cw_oa = [("OC0", oa0), ("OC1", oa1)]
                    e_dr01 = {0: e_read(0), 1: e_read(1)}
                    e_lm01 = {mt: e_mms(mt, list(range(HP - 1)), cw_oa)
                              for mt in range(2)}
                    emit_outer(H - 1)
                    emit_norm_tail(H - 1)
                    emit_outer(H - 2)
                    emit_norm_tail(H - 2)
                    prev_outer_last[0] = outer_c[H - 2]
                    oc0 = p.op("sync", lambda e, osp=osp, HP=HP: e.dma_start(
                        out=o2_sb[0:64, HP - 1, :], in_=osp[:, HP - 1, 0, :]),
                        waits=[("V", mul_c[(H - 2, qh)]), ("OC0", oa0)],
                        inc="OC0", n=16)
                    oc1 = p.op("sync", lambda e, osp=osp, HP=HP: e.dma_start(
                        out=o2_sb[64:128, HP - 1, :], in_=osp[:, HP - 1, 1, :]),
                        waits=[("V", mul_c[(H - 1, qh)]), ("OC1", oa1)],
                        inc="OC1", n=16)
                    cw_oc = [[("OC0", oc0), ("OC1", oc1)]]

                    mul_prev = {0: mul_c[(H - 4, qh)], 1: mul_c[(H - 2, qh)]}
                    prev_av_last[0] = av_c[(H - 1, ST - 1, NC5 - 1)]

                    if qh == NQ - 1:
                        rowtail[r] = {"P": p.cnt["P"], "V": p.cnt["V"]}

                    for mt in range(NMq):
                        drain(1)
                        if mt < 2:
                            lastmm = e_mms(mt, [H // 2 - 1], [])
                            e_tail(mt, e_dr01[mt], lastmm)
                        else:
                            dr_cs = e_read(mt)
                            lastmm = e_mms(mt, list(range(H // 2)), cw_oa)
                            e_tail(mt, dr_cs, lastmm)
                    prev_ev_last[(NMq - 2) % 2] = evE[NMq - 2]
                    prev_ev_last[(NMq - 1) % 2] = evE[NMq - 1]
        drain(len(nxt_qs))
        row_es.close()

    for r in range(R):
        emit_row(r)

    p.op("sync", lambda e: e.nop(),
         waits=[(k, p.cnt[k]) for k in ("OW0", "OW1", "BW0", "BW1", "BR0", "BR1")])
    p.finish()
    p.est.close()
    nc.finalize()
    return nc


def _pick_mat():
    pm = np.zeros((128, 128), dtype=ml_dtypes.bfloat16)
    pm[64, 0:64] = 1
    pm[65, 64:128] = 1
    return pm


def _host_layouts(inputs, n_cores):
    x = np.ascontiguousarray(inputs["L0_attn_x"], dtype=np.float32)
    wq = inputs["L0_attn_wq"]; wk = inputs["L0_attn_wk"]
    wv = inputs["L0_attn_wv"]; wo = inputs["L0_attn_wo"]
    W1 = inputs["L0_ffn_W1"]; W2 = inputs["L0_ffn_W2"]
    D = wq.shape[0]; F = W1.shape[1]
    DT, FT, H = D // 128, F // 128, D // 64
    f32 = lambda a: np.ascontiguousarray(a, np.float32)
    com = {
        "wqT": f32(wq.T),
        "wkT": f32(wk.T),
        "wvT": f32(wv.T),
        "woTr": f32(np.asarray(wo).T.reshape(H // 2, 2, 64, D)
                    .transpose(1, 2, 0, 3).reshape(128, H // 2, D)),
        "w1r": f32(np.asarray(W1).reshape(DT, 128, FT, 128).transpose(1, 2, 0, 3)),
        "w2r": f32(np.asarray(W2).reshape(FT, 128, D).transpose(1, 0, 2)),
        "bqr": f32(np.asarray(inputs["L0_attn_bq"]).reshape(DT, 128).T),
        "bkr": f32(np.asarray(inputs["L0_attn_bk"]).reshape(DT, 128).T),
        "b1r": f32(np.asarray(inputs["L0_ffn_b1"]).reshape(FT, 128).T),
        "bvrep": f32(np.broadcast_to(inputs["L0_attn_bv"], (128, D))),
        "borep": f32(np.broadcast_to(inputs["L0_attn_bo"], (128, D))),
        "b2rep": f32(np.broadcast_to(inputs["L0_ffn_b2"], (128, D))),
        "wrrep": f32(np.broadcast_to(inputs["L0_ffn_w_router"], (128, D))),
        "ident": np.eye(128, dtype=ml_dtypes.bfloat16),
        "pick": _pick_mat(),
    }
    B = x.shape[0]
    R = B // n_cores
    return [dict(com, x=np.ascontiguousarray(x[c * R:(c + 1) * R]))
            for c in range(n_cores)]


def kernel(**inputs):
    x = inputs["L0_attn_x"]
    B, S, D = x.shape
    F = inputs["L0_ffn_W1"].shape[1]
    n_cores = 8
    nc = build_program(S, D, F, B // n_cores)
    in_maps = _host_layouts(inputs, n_cores)
    res = run_bass_kernel_spmd(nc, in_maps, core_ids=list(range(n_cores)))
    return np.concatenate([res.results[c]["out"] for c in range(n_cores)], axis=0)
